# version 07
# baseline (speedup 1.0000x reference)
import math
import numpy as np
import concourse.bass as bass
import concourse.mybir as mybir
from concourse.bass_utils import run_bass_kernel_spmd
import bass_rust

F32 = mybir.dt.float32
BF16 = mybir.dt.bfloat16
U8 = mybir.dt.uint8
AF = mybir.ActivationFunctionType
ALU = mybir.AluOpType
AX = mybir.AxisListType

D = 1024
DH = 4096
H = 16
HD = 64
NF = 128
CH = 128
K0 = 14.0
LSQ = math.log(math.sqrt(NF))
EPS_ATT = 1e-6
EPS_LN = 1e-5
MAGIC = 128.5

FULL = dict(NM=33, NP=31)
T0 = 4096


class TK:
    CSEM = {"sp": None, "pe": None, "dve": None, "act": None, "pool": None}

    DLANES = {"sp": 8, "pool": 4}

    def __init__(self, nc):
        self.nc = nc
        self.eng = {
            "sp": nc.sync, "pe": nc.tensor, "dve": nc.vector,
            "act": nc.scalar, "pool": nc.gpsimd,
        }
        self._cm = []
        self.sem = {}
        names = ["sp", "pe", "dve", "act", "pool"]
        names += [f"dsp{i}" for i in range(self.DLANES["sp"])]
        names += [f"dpl{i}" for i in range(self.DLANES["pool"])]
        for s in names:
            cm = nc.semaphore("sem_" + s)
            h = cm.__enter__()
            self._cm.append(cm)
            self.sem[s] = h
        self.cnt = {s: 0 for s in self.sem}
        self.seen = {e: {s: 0 for s in self.sem} for e in self.eng}
        self.writer = {}
        self.readers = {}
        self.dr = {e: 0 for e in self.eng}
        self.dlane = {"sp": 0, "pool": 0}
        self.bounds = {e: {0} for e in self.eng}
        self.since_bound = {e: 0 for e in self.eng}

    def close(self):
        for cm in reversed(self._cm):
            cm.__exit__(None, None, None)


    def _wait(self, e, s, v):
        self.eng[e].wait_ge(self.sem[s], v)
        self.seen[e][s] = max(self.seen[e][s], v)
        self.bounds[e].add(self.cnt[e])
        self.since_bound[e] = 0

    def _need(self, e, tok, own_dma_sem):
        s, v = tok
        if s == e:
            if e != "pe" and self.dr[e] < v:
                self.eng[e].drain()
                self.dr[e] = self.cnt[e]
            return
        if s in self.eng:
            b = min((x for x in self.bounds[s] if x >= v), default=None)
            if b is None:
                self._wait(s, s, self.cnt[s])
                b = self.cnt[s]
            v = b
        if self.seen[e][s] >= v:
            return
        self._wait(e, s, v)

    LIMIT = None
    NEMIT = 0

    def emit(self, e, fns, reads=(), writes=(), dma=False, label=None):
        TK.NEMIT += 1
        if TK.LIMIT is not None and TK.NEMIT > TK.LIMIT:
            return ("sp", 0)
        deps = []
        for b in reads:
            t = self.writer.get(b)
            if t is not None:
                deps.append(t)
        for b in writes:
            t = self.writer.get(b)
            if t is not None:
                deps.append(t)
            deps.extend(self.readers.get(b, ()))
        best = {}
        for s, v in deps:
            if best.get(s, -1) < v:
                best[s] = v
        for s, v in best.items():
            self._need(e, (s, v), None)
        if dma:
            assert len(fns) == 1, "dma groups must be single-instruction"
            q = "sp" if e == "sp" else "pool"
            pfx = "dsp" if e == "sp" else "dpl"
            lane = self.dlane[q]
            self.dlane[q] = (lane + 1) % self.DLANES[q]
            own = f"{pfx}{lane}"
            if self.seen[e][own] < self.cnt[own]:
                self.eng[e].wait_ge(self.sem[own], self.cnt[own])
                self.seen[e][own] = self.cnt[own]
            ins = fns[0](self.eng[e])
            ins.then_inc(self.sem[own], 16)
            self.cnt[own] += 16
            tok = (own, self.cnt[own])
        else:
            own = e
            h = self.sem[own]
            if self.since_bound[e] >= 3:
                self._wait(e, e, self.cnt[e])
            instrs = []
            for f in fns:
                instrs.append(f(self.eng[e]))
            instrs[-1].then_inc(h, 1)
            self.cnt[own] += 1
            self.since_bound[e] += 1
            tok = (own, self.cnt[own])
        for b in writes:
            self.writer[b] = tok
            self.readers[b] = []
        for b in reads:
            self.readers.setdefault(b, []).append(tok)
        return tok

    def barrier(self):
        for e in self.eng:
            own = e
            self.eng[e].sem_inc(self.sem[own], 1)
            self.cnt[own] += 1
        for e in self.eng:
            if e in self.bounds:
                self.bounds[e].add(self.cnt[e])
        for e in self.eng:
            for s in self.sem:
                if self.cnt[s] > self.seen[e][s]:
                    self._wait(e, s, self.cnt[s])
            self.dr[e] = self.cnt[e]
        self.writer.clear()
        self.readers.clear()


def build_program(nc, NM, NP):
    TM = NM * CH
    TP = NP * CH
    DT = nc.dram_tensor

    x_main = DT("x_main", [TM, D], F32, kind="ExternalInput").ap()
    x_pref = DT("x_pref", [max(TP, 1), D], F32, kind="ExternalInput").ap()
    wqkv_h = DT("wqkv", [D, 3 * D], F32, kind="ExternalInput").ap()
    wo_h = DT("wo", [D, D], F32, kind="ExternalInput").ap()
    wf_h = DT("wfeat2", [128, NF], F32, kind="ExternalInput").ap()
    w1T_h = DT("w1T", [D, DH], F32, kind="ExternalInput").ap()
    w2T_h = DT("w2T", [DH, D], F32, kind="ExternalInput").ap()
    convpk_h = DT("convpk", [128, 128], F32, kind="ExternalInput").ap()
    ident_h = DT("ident", [128, 128], BF16, kind="ExternalInput").ap()
    identf_h = DT("identf", [128, 128], F32, kind="ExternalInput").ap()
    onesf_h = DT("onesf", [128, 128], F32, kind="ExternalInput").ap()
    maskT_h = DT("maskT", [128, 128], BF16, kind="ExternalInput").ap()
    pscale_h = DT("pscale", [128, 1], F32, kind="ExternalInput").ap()
    biasc_h = DT("biasc", [128, 8], F32, kind="ExternalInput").ap()
    y_out = DT("y", [TM, D], F32, kind="ExternalOutput").ap()
    import os as _os
    DBG = _os.environ.get("BASSDBG") == "1"
    dbg = DT("dbg", [16, 128, 2048], F32, kind="ExternalOutput").ap() if DBG else None
    dbg_state = {"used": False}

    def dump(idx, ap_bf16=None, ap_f32=None, rtag=None):
        if not DBG:
            return
        rtags = rtag if isinstance(rtag, list) else [rtag]
        _shp = (ap_bf16 if ap_bf16 is not None else ap_f32).shape
        n = 1
        for _d in _shp[1:]:
            n *= _d
        if ap_bf16 is not None:
            E("pool", [lambda e: e.dma_start(out=dbg[idx, :, 0:n],
                                             in_=ap_bf16)],
              reads=rtags, writes=["dbg_hbm"], dma=True)
        else:
            E("sp", [lambda e: e.dma_start(out=dbg[idx, :, 0:n], in_=ap_f32)],
              reads=rtags, writes=["dbg_hbm"], dma=True)
    x1s = DT("x1s", [TM, D], F32, kind="Internal").ap()
    w1qs = DT("w1qs", [D, DH], BF16, kind="Internal").ap()
    w2qs = DT("w2qs", [DH, D], BF16, kind="Internal").ap()

    xm_t = x_main.rearrange("(c p) d -> p c d", p=CH)
    xp_t = x_pref.rearrange("(c p) d -> p c d", p=CH) if NP else None
    x1s_t = x1s.rearrange("(c p) d -> p c d", p=CH)
    y_t = y_out.rearrange("(c p) d -> p c d", p=CH)

    tk = TK(nc)
    E = tk.emit

    import contextlib
    st = contextlib.ExitStack()
    SB = lambda name, shape, dt: st.enter_context(nc.sbuf_tensor(name, shape, dt))
    PS = lambda name, shape, dt: st.enter_context(nc.psum_tensor(name, shape, dt))

    c_ident = SB("c_ident", [128, 128], BF16)
    c_identf = SB("c_identf", [128, 128], F32)
    c_onesf = SB("c_onesf", [128, 128], F32)
    c_mask = SB("c_mask", [128, 128], BF16)
    c_wf = SB("c_wf", [128, NF], BF16)
    c_conv = SB("c_conv", [128, 128], F32)
    c_ps = SB("c_ps", [128, 1], F32)
    c_b = SB("c_b", [128, 8], F32)
    c_sw = SB("c_sw", [128, 2], F32)

    P = [PS(f"pp{i}", [128, 512], F32) for i in range(8)]
    PB = [p[:] for p in P]
    PBb = [p[:].bitcast(BF16) for p in P]

    for fn, w in [
            (lambda e: e.dma_start(out=c_ident[:], in_=ident_h[:]), "c_ident"),
            (lambda e: e.dma_start(out=c_identf[:], in_=identf_h[:]), "c_identf"),
            (lambda e: e.dma_start(out=c_onesf[:], in_=onesf_h[:]), "c_onesf"),
            (lambda e: e.dma_start(out=c_mask[:], in_=maskT_h[:]), "c_mask"),
            (lambda e: e.dma_start(out=c_conv[:], in_=convpk_h[:]), "c_conv"),
            (lambda e: e.dma_start(out=c_ps[:], in_=pscale_h[:]), "c_ps"),
            (lambda e: e.dma_start(out=c_b[:], in_=biasc_h[:]), "c_b")]:
        E("sp", [fn], writes=[w], dma=True)
    E("pool", [lambda e: e.dma_start(out=c_wf[:], in_=wf_h[:])],
      writes=["c_wf"], dma=True)
    E("act", [lambda e: e.preload_act_table(AF.Exp)], writes=["acttab"])


    stA = contextlib.ExitStack()
    SA = lambda name, shape, dt: stA.enter_context(nc.sbuf_tensor(name, shape, dt))
    a_S = SA("a_S", [128, H * 65], F32)
    a_Sb = SA("a_Sb", [128, H * 65], BF16)
    a_wqkv = SA("a_wqkv", [128, 8, 3 * D], BF16)
    a_wo = SA("a_wo", [128, 8, D], BF16)
    a_x0 = SA("a_x0", [128, 2, D], F32)
    a_xn = SA("a_xn", [128, 2, D], BF16)
    a_xnT = SA("a_xnT", [128, 2, 8, 128], BF16)
    a_q = SA("a_q", [128, 2, D], BF16)
    a_k = SA("a_k", [128, 2, D], BF16)
    a_v = SA("a_v", [128, 2, D], BF16)
    a_qT = SA("a_qT", [128, 2, 8, 128], BF16)
    a_kT = SA("a_kT", [128, 2, 8, 128], BF16)
    a_sq = SA("a_sq", [128, D], F32)
    a_ao = SA("a_ao", [128, D], F32)
    a_st = SA("a_st", [128, 48], F32)
    a_sqq = SA("a_sqq", [128, 2, H], F32)
    a_sqk = SA("a_sqk", [128, 2, H], F32)
    a_ct = SA("a_ct", [128, 2, H], F32)
    a_eps = SA("a_eps", [128, 2, H], F32)
    a_den = SA("a_den", [128, H], F32)
    a_qpT = SA("a_qpT", [128, H, 128], BF16)
    a_kpT = SA("a_kpT", [128, H, 128], BF16)
    a_kp = SA("a_kp", [128, H, 128], BF16)
    a_vp = SA("a_vp", [128, 2, H, 65], BF16)
    a_atm = SA("a_atm", [128, 2, 4, 128], BF16)
    a_y = SA("a_y", [128, D], BF16)
    a_yT = SA("a_yT", [128, 8, 128], BF16)
    a_x1 = SA("a_x1", [128, 2, D], F32)

    for k in range(8):
        E("pool", [lambda e, k=k: e.dma_start(
                    out=a_wqkv[:, k, :], in_=wqkv_h[k * 128:(k + 1) * 128, :])],
          writes=[f"wqkv{k}"], dma=True)
        E("pool", [lambda e, k=k: e.dma_start(
                    out=a_wo[:, k, :], in_=wo_h[k * 128:(k + 1) * 128, :])],
          writes=[f"wo{k}"], dma=True)
    E("dve", [lambda e: e.memset(a_S[:], 0.0)], writes=["a_S"])
    E("dve", [lambda e: e.memset(a_Sb[:], 0.0)], writes=["a_Sb"])

    q_in = SA("q_in", [128, 2, DH], F32)
    q_t = SA("q_t", [128, DH // 2], F32)
    q_u = SA("q_u", [128, DH // 2], U8)
    q_d = SA("q_d", [128, DH // 2], U8)
    q_bf = SA("q_bf", [128, DH // 2], BF16)
    q_acc = SA("q_acc", [128, 2], F32)
    q_s = SA("q_s", [128, 2], F32)
    qsteps = []

    def _q_mean(wi, w_h, n_r, n_c):
        nch = n_r // 128
        yield lambda: E("dve", [lambda e: e.memset(q_acc[:, wi:wi + 1], 0.0)],
                        writes=[f"q_acc{wi}"])
        for ci in range(nch):
            def s(ci=ci, sl=ci % 2):
                E("sp", [lambda e: e.dma_start(out=q_in[:, sl, 0:n_c],
                                               in_=w_h[ci * 128:(ci + 1) * 128, :])],
                  writes=[f"q_in{sl}"], dma=True)
                E("dve", [lambda e: e.reduce_sum(out=q_t[:, 0:1],
                                                 in_=q_in[:, sl, 0:n_c],
                                                 axis=AX.X,
                                                 apply_absolute_value=True)],
                  reads=[f"q_in{sl}"], writes=["q_t"])
                E("dve", [lambda e: e.tensor_tensor(out=q_acc[:, wi:wi + 1],
                                                    in0=q_acc[:, wi:wi + 1],
                                                    in1=q_t[:, 0:1],
                                                    op=ALU.add)],
                  reads=["q_t", f"q_acc{wi}"], writes=[f"q_acc{wi}"])
            yield s
        def fin():
            E("pe", [lambda e: e.matmul(
                        PB[7][0:1, 0:1], c_onesf[:, 0:1], q_acc[:, wi:wi + 1],
                        start=True, stop=True)],
              reads=[f"q_acc{wi}", "c_onesf"], writes=["P7"])
            E("dve", [lambda e: e.tensor_copy(q_s[0:1, wi:wi + 1],
                                              PB[7][0:1, 0:1])],
              reads=["P7"], writes=[f"q_sr{wi}"])
            E("pe", [lambda e: e.matmul(
                        PB[7][:, 0:1], c_onesf[0:1, :], q_s[0:1, wi:wi + 1],
                        start=True, stop=True)],
              reads=[f"q_sr{wi}", "c_onesf"], writes=["P7"])
            E("dve", [lambda e: e.tensor_copy(q_s[:, wi:wi + 1],
                                              PB[7][:, 0:1])],
              reads=["P7"], writes=[f"q_s{wi}"])
            E("dve", [lambda e, n=(n_r * n_c): e.tensor_scalar(
                        out=c_sw[:, wi:wi + 1], in0=q_s[:, wi:wi + 1],
                        scalar1=1.0 / n, scalar2=1e-5,
                        op0=ALU.mult, op1=ALU.max)],
              reads=[f"q_s{wi}"], writes=[f"c_sw{wi}"])
            E("dve", [lambda e: e.reciprocal(out=q_s[:, wi:wi + 1],
                                             in_=c_sw[:, wi:wi + 1])],
              reads=[f"c_sw{wi}"], writes=[f"q_s{wi}"])
        yield fin

    def _q_quant(wi, w_h, wq_h, n_r, n_c):
        nch = n_r // 128
        for ci in range(nch):
            def s(ci=ci, sl=ci % 2):
                E("sp", [lambda e: e.dma_start(out=q_in[:, sl, 0:n_c],
                                               in_=w_h[ci * 128:(ci + 1) * 128, :])],
                  writes=[f"q_in{sl}"], dma=True)
                for hf in range(0, n_c, DH // 2):
                    nn = min(DH // 2, n_c - hf)
                    E("act", [lambda e, hf=hf, nn=nn, sl=sl: e.activation(
                                out=q_t[:, 0:nn], in_=q_in[:, sl, hf:hf + nn],
                                func=AF.Identity, bias=c_b[:, 4:5],
                                scale=q_s[:, wi:wi + 1])],
                      reads=[f"q_in{sl}", f"q_s{wi}", "c_b"], writes=["q_t"])
                    E("act", [lambda e, nn=nn: e.copy(q_u[:, 0:nn],
                                                      q_t[:, 0:nn])],
                      reads=["q_t"], writes=["q_u"])
                    E("dve", [lambda e, nn=nn: e.tensor_tensor(
                                out=q_d[:, 0:nn], in0=q_u[:, 0:nn],
                                in1=q_t[:, 0:nn], op=ALU.is_gt)],
                      reads=["q_u", "q_t"], writes=["q_d"])
                    E("dve", [lambda e, nn=nn: e.tensor_tensor(
                                out=q_t[:, 0:nn], in0=q_u[:, 0:nn],
                                in1=q_d[:, 0:nn], op=ALU.subtract)],
                      reads=["q_u", "q_d"], writes=["q_t"])
                    E("dve", [lambda e, nn=nn: e.tensor_scalar(
                                out=q_t[:, 0:nn], in0=q_t[:, 0:nn],
                                scalar1=-128.0, scalar2=-1.0,
                                op0=ALU.add, op1=ALU.max)],
                      reads=["q_t"], writes=["q_t"])
                    E("dve", [lambda e, nn=nn: e.tensor_scalar(
                                out=q_bf[:, 0:nn], in0=q_t[:, 0:nn],
                                scalar1=1.0, scalar2=None, op0=ALU.min)],
                      reads=["q_t"], writes=["q_bf"])
                    E("sp", [lambda e, ci=ci, hf=hf, nn=nn: e.dma_start(
                                out=wq_h[ci * 128:(ci + 1) * 128, hf:hf + nn],
                                in_=q_bf[:, 0:nn])],
                      reads=["q_bf"], writes=["wq_hbm"], dma=True)
            yield s

    for g in (_q_mean(0, w1T_h, D, DH), _q_mean(1, w2T_h, DH, D)):
        qsteps.extend(g)
    for g in (_q_quant(0, w1T_h, w1qs, D, DH), _q_quant(1, w2T_h, w2qs, DH, D)):
        qsteps.extend(g)
    qsteps.reverse()

    def ln_stats(xin_ap, rbufs, cb):
        def C(i):
            return a_st[:, cb + i:cb + i + 1]
        E("dve", [lambda e: e.bn_stats(a_st[:, cb + 8:cb + 14],
                                       xin_ap[:, 0:512])],
          reads=rbufs, writes=[f"st{cb+8}"])
        E("dve", [lambda e: e.bn_stats(a_st[:, cb + 14:cb + 20],
                                       xin_ap[:, 512:1024])],
          reads=rbufs, writes=[f"st{cb+14}"])
        E("dve", [lambda e: e.bn_aggr(a_st[:, cb + 2:cb + 4],
                                      a_st[:, cb + 8:cb + 20])],
          reads=[f"st{cb+8}", f"st{cb+14}"], writes=[f"st{cb+2}", f"st{cb+3}"])
        if _os.environ.get("GLNEXP") == "1":
            E("act", [lambda e: e.activation(out=C(4), in_=C(3),
                                             func=AF.Ln, bias=c_b[:, 0:1],
                                             scale=1.0)],
              reads=[f"st{cb+3}", "c_b"], writes=[f"st{cb+4}"])
            E("act", [lambda e: e.activation(out=C(4), in_=C(4),
                                             func=AF.Exp, scale=-0.5)],
              reads=[f"st{cb+4}"], writes=[f"st{cb+4}"])
            return
        E("dve", [lambda e: e.tensor_scalar(out=C(6), in0=C(3),
                                            scalar1=EPS_LN, scalar2=None,
                                            op0=ALU.add)],
          reads=[f"st{cb+3}"], writes=[f"st{cb+6}"])
        E("dve", [lambda e: e.tensor_scalar(
                    out=C(4).bitcast(mybir.dt.int32)[:, 0:1],
                    in0=C(6).bitcast(mybir.dt.int32)[:, 0:1],
                    scalar1=1, scalar2=-1, op0=ALU.logical_shift_right,
                    op1=ALU.bitwise_xor)],
          reads=[f"st{cb+6}"], writes=[f"st{cb+4}"])
        E("dve", [lambda e: e.tensor_scalar(
                    out=C(4).bitcast(mybir.dt.int32)[:, 0:1],
                    in0=C(4).bitcast(mybir.dt.int32)[:, 0:1],
                    scalar1=0x5f3759e0, scalar2=None, op0=ALU.add)],
          reads=[f"st{cb+4}"], writes=[f"st{cb+4}"])
        for _ in range(3):
            E("dve", [lambda e: e.tensor_tensor(out=C(7), in0=C(4), in1=C(4),
                                                op=ALU.mult)],
              reads=[f"st{cb+4}"], writes=[f"st{cb+7}"])
            E("dve", [lambda e: e.tensor_tensor(out=C(7), in0=C(7), in1=C(6),
                                                op=ALU.mult)],
              reads=[f"st{cb+7}", f"st{cb+6}"], writes=[f"st{cb+7}"])
            E("dve", [lambda e: e.tensor_scalar(out=C(7), in0=C(7),
                                                scalar1=-0.5, scalar2=1.5,
                                                op0=ALU.mult, op1=ALU.add)],
              reads=[f"st{cb+7}"], writes=[f"st{cb+7}"])
            E("dve", [lambda e: e.tensor_tensor(out=C(4), in0=C(4), in1=C(7),
                                                op=ALU.mult)],
              reads=[f"st{cb+4}", f"st{cb+7}"], writes=[f"st{cb+4}"])

    def transpose8(src_ap, src_tag, dst_ap, dst_tag, bank):
        fns = []
        for k in range(8):
            fns.append(lambda e, k=k: e.transpose(
                PBb[bank][:, k * 128:(k + 1) * 128],
                src_ap[:, k * 128:(k + 1) * 128], c_ident[:]))
        E("pe", fns, reads=[src_tag, "c_ident"], writes=[f"P{bank}"])
        E("act", [lambda e: e.copy(
                    dst_ap.rearrange("p a b -> p (a b)"), PBb[bank][:])],
          reads=[f"P{bank}"], writes=[dst_tag])

    def front(cidx, is_prefix, x_t, sl):
        xt = f"x0.{sl}"
        E("sp", [lambda e: e.dma_start(out=a_x0[:, sl, :],
                                       in_=x_t[:, cidx, :])],
          writes=[xt], dma=True)
        ln_stats(a_x0[:, sl, :], [xt], 0)
        E("dve", [lambda e: e.tensor_tensor(
                    out=a_st[:, 5:6], in0=a_st[:, 2:3], in1=a_st[:, 4:5],
                    op=ALU.mult)],
          reads=["st2", "st4"], writes=["st5"])
        E("dve", [lambda e: e.tensor_scalar(
                    out=a_st[:, 5:6], in0=a_st[:, 5:6],
                    scalar1=-1.0, scalar2=None, op0=ALU.mult)],
          reads=["st5"], writes=["st5"])
        E("act", [lambda e: e.activation(
                    out=a_xn[:, sl, :], in_=a_x0[:, sl, :],
                    func=AF.Identity, bias=a_st[:, 5:6],
                    scale=a_st[:, 4:5])],
          reads=[xt, "st5", "st4"], writes=[f"a_xn{sl}"])
        transpose8(a_xn[:, sl, :], f"a_xn{sl}", a_xnT[:, sl], f"a_xnT{sl}", 2)

        groups = [(0, a_q), (1, a_k), (2, a_v)]
        names = ["a_q", "a_k", "a_v"]
        if is_prefix:
            groups = groups[1:]
        for g, qsb in groups:
            qtag = f"{names[g]}{sl}"
            for half in range(2):
                bank = half
                fns = []
                for k in range(8):
                    fns.append(lambda e, k=k, sl=sl: e.ldweights(
                        a_xnT[:, sl, k, :]))
                    fns.append(lambda e, k=k, g=g, half=half, bank=bank, sl=sl:
                               e.matmul(PB[bank][:],
                                        a_xnT[:, sl, k, :],
                                        a_wqkv[:, k, g * D + half * 512:
                                               g * D + half * 512 + 512],
                                        start=(k == 0), stop=(k == 7)))
                E("pe", fns,
                  reads=[f"a_xnT{sl}"] + [f"wqkv{k}" for k in range(8)],
                  writes=[f"P{bank}"])
            if g in (0, 1):
                for half in range(2):
                    E("act", [lambda e, half=half: e.activation(
                                out=a_sq[:, half * 512:half * 512 + 512],
                                in_=PB[half][:], func=AF.Square)],
                      reads=[f"P{half}"], writes=["a_sq"])
                dst = a_sqq if g == 0 else a_sqk
                dtag = ("a_sqq" if g == 0 else "a_sqk") + str(sl)
                E("dve", [lambda e, dst=dst, sl=sl: e.tensor_reduce(
                            out=dst[:, sl], in_=a_sq[:].rearrange(
                                "p (h d) -> p h d", h=H),
                            axis=AX.X, op=ALU.add)],
                  reads=["a_sq"], writes=[dtag])
            for half in range(2):
                E("act", [lambda e, half=half, qsb=qsb, sl=sl: e.copy(
                            qsb[:, sl, half * 512:half * 512 + 512],
                            PB[half][:])],
                  reads=[f"P{half}"], writes=[qtag])

        if not is_prefix:
            transpose8(a_q[:, sl, :], f"a_q{sl}", a_qT[:, sl], f"a_qT{sl}", 2)
        transpose8(a_k[:, sl, :], f"a_k{sl}", a_kT[:, sl], f"a_kT{sl}", 2)

        E("act", [lambda e, sl=sl: e.activation(out=a_ct[:, sl],
                                                in_=a_sqk[:, sl],
                                                func=AF.Exp, bias=c_b[:, 1:2],
                                                scale=-0.5)],
          reads=[f"a_sqk{sl}", "c_b"], writes=[f"a_ct{sl}"])
        if not is_prefix:
            E("act", [lambda e, sl=sl: e.activation(out=a_eps[:, sl],
                                                    in_=a_sqq[:, sl],
                                                    func=AF.Exp,
                                                    bias=c_b[:, 2:3],
                                                    scale=0.5)],
              reads=[f"a_sqq{sl}", "c_b"], writes=[f"a_eps{sl}"])

        E("dve", [lambda e, sl=sl: e.tensor_tensor(
                    out=a_vp[:, sl, :, 0:64],
                    in0=a_v[:, sl, :].rearrange("p (h d) -> p h d", h=H),
                    in1=a_ct[:, sl].unsqueeze(2).broadcast_to([128, H, 64]),
                    op=ALU.mult)],
          reads=[f"a_v{sl}", f"a_ct{sl}"], writes=[f"a_vp{sl}"])
        E("dve", [lambda e, sl=sl: e.tensor_copy(a_vp[:, sl, :, 64:65],
                                                 a_ct[:, sl].unsqueeze(2))],
          reads=[f"a_ct{sl}"], writes=[f"a_vp{sl}"])

    HGRP = [[0, 2, 4, 6], [8, 10, 12, 14], [1, 3, 5, 7], [9, 11, 13, 15]]

    def featurize(dst_sb, dst_tag, src_sb, src_tag, tok_major):
        for heads in HGRP:
            bank = nb()
            po = (heads[0] % 2) * 64
            fns = []
            for hh, h in enumerate(heads):
                ko = h // 2
                if tok_major:
                    fns.append(lambda e, ko=ko, po=po: e.ldweights(
                        src_sb[po:po + 64, ko, :], tile_position=(po, 0)))
                    fns.append(lambda e, ko=ko, po=po, hh=hh: e.matmul(
                        PB[bank][:, hh * 128:hh * 128 + 128],
                        src_sb[po:po + 64, ko, :],
                        c_wf[po:po + 64, :],
                        start=(hh == 0), stop=(hh == 3)))
                else:
                    fns.append(lambda e, po=po: e.ldweights(
                        c_wf[po:po + 64, :], tile_position=(po, 0)))
                    fns.append(lambda e, ko=ko, po=po, hh=hh: e.matmul(
                        PB[bank][:, hh * 128:hh * 128 + 128],
                        c_wf[po:po + 64, :],
                        src_sb[po:po + 64, ko, :],
                        start=(hh == 0), stop=(hh == 3)))
            E("pe", fns, reads=[src_tag, "c_wf"], writes=[f"P{bank}"])
            E("act", [lambda e, heads=heads, bank=bank, dst_sb=dst_sb:
                      e.activation(
                        out=dst_sb[:, heads[0]:heads[3] + 1:2, :],
                        in_=PB[bank][:].rearrange("p (g f) -> p g f", g=4),
                        func=AF.Exp, bias=c_b[:, 3:4], scale=1.0)],
              reads=[f"P{bank}"], writes=[dst_tag])

    rot = {"i": 0}

    def nb():
        rot["i"] += 1
        return 3 + (rot["i"] % 5)

    def back(cidx, is_prefix, sl):
        featurize(a_kp, "a_kp", a_kT[:, sl], f"a_kT{sl}", True)
        featurize(a_kpT, "a_kpT", a_kT[:, sl], f"a_kT{sl}", False)
        if not is_prefix:
            featurize(a_qpT, "a_qpT", a_qT[:, sl], f"a_qT{sl}", False)

            for hg in range(4):
                bA = nb()
                bO = nb()
                fns = []
                for hh in range(4):
                    h = hg * 4 + hh
                    fns.append(lambda e, h=h: e.ldweights(a_kpT[:, h, :]))
                    fns.append(lambda e, h=h, hh=hh, bA=bA: e.matmul(
                        PB[bA][:, hh * 128:hh * 128 + 128],
                        a_kpT[:, h, :], a_qpT[:, h, :],
                        start=(hh == 0), stop=(hh == 3)))
                E("pe", fns, reads=["a_kpT", "a_qpT"], writes=[f"P{bA}"])
                E("dve", [lambda e, bA=bA, hg=hg: e.tensor_tensor(
                            out=a_atm[:, hg % 2],
                            in0=PB[bA][:].rearrange("p (g f) -> p g f", g=4),
                            in1=c_mask[:].unsqueeze(1).broadcast_to(
                                [128, 4, 128]),
                            op=ALU.mult)],
                  reads=[f"P{bA}", "c_mask"], writes=[f"a_atm{hg % 2}"])
                fns = []
                for hh in range(4):
                    h = hg * 4 + hh
                    fns.append(lambda e, hh=hh, hg=hg: e.ldweights(
                        a_atm[:, hg % 2, hh, :]))
                    fns.append(lambda e, h=h, hh=hh, sl=sl, bO=bO, hg=hg:
                               e.matmul(
                        PB[bO][:, hh * 65:hh * 65 + 65],
                        a_atm[:, hg % 2, hh, :], a_vp[:, sl, h, :],
                        start=(hh == 0), stop=False))
                    fns.append(lambda e, h=h: e.ldweights(a_qpT[:, h, :]))
                    fns.append(lambda e, h=h, hh=hh, bO=bO: e.matmul(
                        PB[bO][:, hh * 65:hh * 65 + 65],
                        a_qpT[:, h, :],
                        a_Sb[:, h * 65:h * 65 + 65],
                        start=False, stop=(hh == 3)))
                E("pe", fns,
                  reads=[f"a_atm{hg % 2}", f"a_vp{sl}", "a_qpT", "a_Sb"],
                  writes=[f"P{bO}"])
                E("dve", [lambda e, hg=hg, sl=sl, bO=bO: e.tensor_tensor(
                            out=a_den[:, hg * 4:hg * 4 + 4].unsqueeze(2),
                            in0=PB[bO][:, 0:260].rearrange(
                                "p (g f) -> p g f", g=4)[:, :, 64:65],
                            in1=a_eps[:, sl, hg * 4:hg * 4 + 4].unsqueeze(2),
                            op=ALU.add)],
                  reads=[f"P{bO}", f"a_eps{sl}"], writes=["a_den"])
                E("dve", [lambda e, hg=hg: e.reciprocal(
                            out=a_den[:, hg * 4:hg * 4 + 4],
                            in_=a_den[:, hg * 4:hg * 4 + 4])],
                  reads=["a_den"], writes=["a_den"])
                E("dve", [lambda e, hg=hg, bO=bO: e.tensor_tensor(
                            out=a_y[:].rearrange("p (h d) -> p h d", h=H)
                                [:, hg * 4:hg * 4 + 4, :],
                            in0=PB[bO][:, 0:260].rearrange(
                                "p (g f) -> p g f", g=4)[:, :, 0:64],
                            in1=a_den[:, hg * 4:hg * 4 + 4].unsqueeze(2)
                                .broadcast_to([128, 4, 64]),
                            op=ALU.mult)],
                  reads=[f"P{bO}", "a_den"], writes=["a_y"])

        for hg in range(4):
            bD = nb()
            fns = []
            for hh in range(4):
                h = hg * 4 + hh
                fns.append(lambda e, h=h: e.ldweights(a_kp[:, h, :]))
                fns.append(lambda e, h=h, hh=hh, sl=sl, bD=bD: e.matmul(
                    PB[bD][:, hh * 65:hh * 65 + 65],
                    a_kp[:, h, :], a_vp[:, sl, h, :],
                    start=(hh == 0), stop=(hh == 3)))
            E("pe", fns, reads=["a_kp", f"a_vp{sl}"], writes=[f"P{bD}"])
            s_lo, s_hi = hg * 4 * 65, (hg * 4 + 4) * 65
            if is_prefix:
                E("dve", [lambda e, s_lo=s_lo, s_hi=s_hi, bD=bD:
                          e.scalar_tensor_tensor(
                            out=a_S[:, s_lo:s_hi], in0=PB[bD][:, 0:260],
                            scalar=c_ps[:, 0:1], in1=a_S[:, s_lo:s_hi],
                            op0=ALU.mult, op1=ALU.add)],
                  reads=[f"P{bD}", "c_ps", "a_S"], writes=["a_S"])
            else:
                E("dve", [lambda e, s_lo=s_lo, s_hi=s_hi, bD=bD: e.tensor_tensor(
                            out=a_S[:, s_lo:s_hi], in0=PB[bD][:, 0:260],
                            in1=a_S[:, s_lo:s_hi], op=ALU.add)],
                  reads=[f"P{bD}", "a_S"], writes=["a_S"])
        E("act", [lambda e: e.copy(a_Sb[:], a_S[:])],
          reads=["a_S"], writes=["a_Sb"])

        if is_prefix:
            return

        transpose8(a_y[:], "a_y", a_yT[:], "a_yT", nb())
        wob = [nb(), nb()]
        for half in range(2):
            bank = wob[half]
            fns = []
            for k in range(8):
                fns.append(lambda e, k=k: e.ldweights(a_yT[:, k, :]))
                fns.append(lambda e, k=k, half=half, bank=bank: e.matmul(
                    PB[bank][:], a_yT[:, k, :],
                    a_wo[:, k, half * 512:half * 512 + 512],
                    start=(k == 0), stop=(k == 7)))
            E("pe", fns, reads=["a_yT"] + [f"wo{k}" for k in range(8)],
              writes=[f"P{bank}"])
        for half in range(2):
            bank = wob[half]
            E("act", [lambda e, half=half, bank=bank: e.copy(
                        a_ao[:, half * 512:half * 512 + 512], PB[bank][:])],
              reads=[f"P{bank}"], writes=["a_ao"])
        ln_stats(a_ao[:, 0:D], ["a_ao"], 24)
        E("dve", [lambda e, sl=sl: e.tensor_scalar(
                    out=a_x1[:, sl, :], in0=a_ao[:, 0:D],
                    scalar1=a_st[:, 26:27], scalar2=a_st[:, 28:29],
                    op0=ALU.subtract, op1=ALU.mult)],
          reads=["a_ao", "st26", "st28"], writes=[f"x1.{sl}"])
        E("dve", [lambda e, sl=sl: e.tensor_tensor(
                    out=a_x1[:, sl, :], in0=a_x1[:, sl, :],
                    in1=a_x0[:, sl, :], op=ALU.add)],
          reads=[f"x0.{sl}", f"x1.{sl}"], writes=[f"x1.{sl}"])
        E("sp", [lambda e, cidx=cidx, sl=sl: e.dma_start(
                    out=x1s_t[:, cidx, :], in_=a_x1[:, sl, :])],
          reads=[f"x1.{sl}"], writes=["x1s_hbm"], dma=True)

    seq = [(c, True, xp_t) for c in range(NP)] + \
          [(c, False, xm_t) for c in range(NM)]
    if _os.environ.get("SKIPA") == "1":
        seq = seq[0:1]
    front(seq[0][0], seq[0][1], seq[0][2], 0)
    for i in range(len(seq)):
        for _ in range(2):
            if qsteps:
                qsteps.pop()()
        if i + 1 < len(seq):
            c2, p2, x2 = seq[i + 1]
            front(c2, p2, x2, (i + 1) % 2)
        c1, p1, _ = seq[i]
        back(c1, p1, i % 2)
    while qsteps:
        qsteps.pop()()


    tk.barrier()
    stA.close()

    f_w1 = SB("f_w1", [128, 8, DH], BF16)
    f_w2 = SB("f_w2", [128, 32, D], BF16)
    f_x1 = SB("f_x1", [128, 4, D], F32)
    f_xn = SB("f_xn", [128, D], F32)
    f_xq1 = SB("f_xq1", [128, D], BF16)
    f_xq1T = SB("f_xq1T", [128, 8, 4, 128], BF16)
    f_xq1Tf = f_xq1T[:].rearrange("p a s t -> p a (s t)")
    f_h4 = SB("f_h4", [128, 32, 256], F32)
    f_tail = SB("f_tail", [128, 32], F32)
    f_cv = SB("f_cv", [128, 2, 257], F32)
    f_M = SB("f_M", [128, 256], F32)
    f_s2 = SB("f_s2", [128, 256], F32)
    f_row = SB("f_row", [128, 256], F32)
    f_tok = SB("f_tok", [128, 8], F32)
    f_xq2 = SB("f_xq2", [128, 2, 256], BF16)
    f_qs = SB("f_qs", [128, 2, 256], F32)
    f_qu = SB("f_qu", [128, 2, 256], U8)
    f_qd = SB("f_qd", [128, 2, 256], U8)
    f_y2 = SB("f_y2", [128, D], F32)
    f_st = SB("f_st", [128, 24], F32)
    f_u8 = SB("f_u8", [128, D], U8)
    f_u8b = SB("f_u8b", [128, D], U8)

    E("act", [lambda e: e.preload_act_table(AF.Silu)], writes=["acttab"])
    for k in range(8):
        E("sp", [lambda e, k=k: e.dma_start(
                    out=f_w1[:, k, :], in_=w1qs[k * 128:(k + 1) * 128, :])],
          reads=["wq_hbm"], writes=[f"fw1{k}"], dma=True)
    for m in range(32):
        E("sp", [lambda e, m=m: e.dma_start(
                    out=f_w2[:, m, :], in_=w2qs[m * 128:(m + 1) * 128, :])],
          reads=["wq_hbm"], writes=[f"fw2{m}"], dma=True)
    E("dve", [lambda e: e.memset(f_tail[:], 0.0)], writes=["f_tail"])

    def fln_stats(xin_ap, rbufs, dst_mean, dst_rstd):
        E("dve", [lambda e: e.bn_stats(f_st[:, 8:14], xin_ap[:, 0:512])],
          reads=rbufs, writes=["fst8"])
        E("dve", [lambda e: e.bn_stats(f_st[:, 14:20], xin_ap[:, 512:1024])],
          reads=rbufs, writes=["fst14"])
        E("dve", [lambda e: e.bn_aggr(f_st[:, 2:4], f_st[:, 8:20])],
          reads=["fst8", "fst14"], writes=["fstm", "fst3"])
        E("dve", [lambda e: e.tensor_scalar(out=f_st[:, 4:5], in0=f_st[:, 3:4],
                                            scalar1=EPS_LN, scalar2=None,
                                            op0=ALU.add)],
          reads=["fst3"], writes=["fst4"])
        E("dve", [lambda e: e.tensor_scalar(
                    out=f_st[:, 5:6].bitcast(mybir.dt.int32)[:, 0:1],
                    in0=f_st[:, 4:5].bitcast(mybir.dt.int32)[:, 0:1],
                    scalar1=1, scalar2=-1, op0=ALU.logical_shift_right,
                    op1=ALU.bitwise_xor)],
          reads=["fst4"], writes=["fst5"])
        E("dve", [lambda e: e.tensor_scalar(
                    out=f_st[:, 5:6].bitcast(mybir.dt.int32)[:, 0:1],
                    in0=f_st[:, 5:6].bitcast(mybir.dt.int32)[:, 0:1],
                    scalar1=0x5f3759e0, scalar2=None, op0=ALU.add)],
          reads=["fst5"], writes=["fst5"])
        for _ in range(3):
            E("dve", [lambda e: e.tensor_tensor(out=f_st[:, 6:7],
                                                in0=f_st[:, 5:6],
                                                in1=f_st[:, 5:6],
                                                op=ALU.mult)],
              reads=["fst5"], writes=["fst6"])
            E("dve", [lambda e: e.tensor_tensor(out=f_st[:, 6:7],
                                                in0=f_st[:, 6:7],
                                                in1=f_st[:, 4:5],
                                                op=ALU.mult)],
              reads=["fst6", "fst4"], writes=["fst6"])
            E("dve", [lambda e: e.tensor_scalar(out=f_st[:, 6:7],
                                                in0=f_st[:, 6:7],
                                                scalar1=-0.5, scalar2=1.5,
                                                op0=ALU.mult, op1=ALU.add)],
              reads=["fst6"], writes=["fst6"])
            E("dve", [lambda e: e.tensor_tensor(out=f_st[:, 5:6],
                                                in0=f_st[:, 5:6],
                                                in1=f_st[:, 6:7],
                                                op=ALU.mult)],
              reads=["fst5", "fst6"], writes=["fst5"])
        E("dve", [lambda e: e.tensor_copy(f_st[:, dst_rstd:dst_rstd + 1],
                                          f_st[:, 5:6])],
          reads=["fst5"], writes=["fstr"])

    n_tiles = (NM + 1) // 2

    def prep_chunk(cidx):
        slot = cidx % 4
        xsl = cidx % 4
        E("sp", [lambda e, cidx=cidx, xsl=xsl: e.dma_start(
                    out=f_x1[:, xsl, :], in_=x1s_t[:, cidx, :])],
          reads=["x1s_hbm"], writes=[f"fx1.{xsl}"], dma=True)
        fln_stats(f_x1[:, xsl, :], [f"fx1.{xsl}"], 2, 7)
        E("dve", [lambda e, xsl=xsl: e.tensor_scalar(
                    out=f_xn[:], in0=f_x1[:, xsl, :],
                    scalar1=f_st[:, 2:3], scalar2=f_st[:, 7:8],
                    op0=ALU.subtract, op1=ALU.mult)],
          reads=[f"fx1.{xsl}", "fstm", "fstr"], writes=["f_xn"])
        E("dve", [lambda e: e.tensor_reduce(out=f_st[:, 8:9], in_=f_xn[:, 0:D],
                                            axis=AX.X, op=ALU.max,
                                            apply_absolute_value=True)],
          reads=["f_xn"], writes=["fst8"])
        E("dve", [lambda e: e.tensor_scalar(out=f_st[:, 8:9], in0=f_st[:, 8:9],
                                            scalar1=1e-5, scalar2=None,
                                            op0=ALU.max)],
          reads=["fst8"], writes=["fst8"])
        E("dve", [lambda e: e.reciprocal(out=f_st[:, 9:10], in_=f_st[:, 8:9])],
          reads=["fst8"], writes=["fst9"])
        E("dve", [lambda e: e.tensor_scalar(out=f_st[:, 9:10],
                                            in0=f_st[:, 9:10],
                                            scalar1=127.0, scalar2=None,
                                            op0=ALU.mult)],
          reads=["fst9"], writes=["fst9"])
        E("dve", [lambda e: e.tensor_scalar(out=f_st[:, 10:11],
                                            in0=f_st[:, 8:9],
                                            scalar1=c_sw[:, 0:1],
                                            scalar2=1.0 / 127.0,
                                            op0=ALU.mult, op1=ALU.mult)],
          reads=["fst8", "c_sw0"], writes=["fst10"])
        E("act", [lambda e: e.activation(out=f_xn[:], in_=f_xn[:],
                                         func=AF.Identity,
                                         bias=c_b[:, 4:5],
                                         scale=f_st[:, 9:10])],
          reads=["f_xn", "fst9", "c_b"], writes=["f_xn"])
        E("act", [lambda e: e.copy(f_u8[:], f_xn[:])],
          reads=["f_xn"], writes=["f_u8"])
        E("dve", [lambda e: e.tensor_tensor(out=f_u8b[:], in0=f_u8[:],
                                            in1=f_xn[:], op=ALU.is_gt)],
          reads=["f_u8", "f_xn"], writes=["f_u8b"])
        E("dve", [lambda e: e.tensor_tensor(out=f_xn[:], in0=f_u8[:],
                                            in1=f_u8b[:], op=ALU.subtract)],
          reads=["f_u8", "f_u8b"], writes=["f_xn"])
        E("dve", [lambda e: e.tensor_scalar(out=f_xq1[:], in0=f_xn[:],
                                            scalar1=-128.0,
                                            scalar2=f_st[:, 10:11],
                                            op0=ALU.add, op1=ALU.mult)],
          reads=["f_xn", "fst10"], writes=["f_xq1"])
        fns = []
        for k in range(8):
            fns.append(lambda e, k=k: e.transpose(
                PBb[2][:, k * 128:(k + 1) * 128],
                f_xq1[:, k * 128:(k + 1) * 128], c_ident[:]))
        E("pe", fns, reads=["f_xq1", "c_ident"], writes=["P2"])
        E("dve", [lambda e, slot=slot: e.tensor_copy(
                    f_xq1T[:, :, slot, :],
                    PBb[2][:].rearrange("p (a b) -> p a b", a=8))],
          reads=["P2"], writes=[f"xq1T.{slot}"])

    def ffn_tile(ft):
        c0 = ft * 2
        ntok = 256 if c0 + 1 < NM else 128
        nch = 2 if ntok == 256 else 1
        slot0 = c0 % 4
        halo_slot = (c0 + 2) % 4 if c0 + 2 < NM else None
        rbufs = [f"xq1T.{slot0}"] + ([f"xq1T.{slot0 + 1}"] if nch == 2 else [])
        if halo_slot is not None:
            rbufs.append(f"xq1T.{halo_slot}")
        for m in range(32):
            hb = m % 3
            fns = []
            for k in range(8):
                fns.append(lambda e, k=k, m=m: e.ldweights(
                    f_w1[:, k, m * 128:(m + 1) * 128]))
                fns.append(lambda e, k=k, m=m, hb=hb, slot0=slot0, ntok=ntok:
                           e.matmul(PB[hb][:, 0:ntok],
                                    f_w1[:, k, m * 128:(m + 1) * 128],
                                    f_xq1Tf[:, k, slot0 * 128:
                                            slot0 * 128 + ntok],
                                    start=(k == 0), stop=(k == 7 and
                                                          halo_slot is None)))
            if halo_slot is not None:
                for k in range(8):
                    fns.append(lambda e, k=k, m=m, hb=hb, halo_slot=halo_slot,
                               ntok=ntok: e.matmul(
                                   PB[hb][:, ntok:ntok + 1],
                                   f_w1[:, k, m * 128:(m + 1) * 128],
                                   f_xq1Tf[:, k, halo_slot * 128:
                                           halo_slot * 128 + 1],
                                   start=False, stop=(k == 7)))
            E("pe", fns, reads=rbufs + [f"fw1{k}" for k in range(8)],
              writes=[f"P{hb}"])
            nconv = ntok + (1 if halo_slot is not None else 0)
            E("act", [lambda e, m=m, hb=hb, nconv=nconv: e.activation(
                        out=f_cv[:, 0, 0:nconv], in_=PB[hb][:, 0:nconv],
                        func=AF.Identity, bias=c_conv[:, m * 4 + 3:m * 4 + 4],
                        scale=c_conv[:, m * 4 + 1:m * 4 + 2])],
              reads=[f"P{hb}", "c_conv"], writes=["f_cv0"])
            E("dve", [lambda e, m=m, hb=hb, ntok=ntok: e.scalar_tensor_tensor(
                        out=f_cv[:, 1, 1:ntok], in0=PB[hb][:, 0:ntok - 1],
                        scalar=c_conv[:, m * 4:m * 4 + 1],
                        in1=f_cv[:, 0, 1:ntok], op0=ALU.mult, op1=ALU.add)],
              reads=[f"P{hb}", "c_conv", "f_cv0"], writes=["f_cv1"])
            E("dve", [lambda e, m=m: e.scalar_tensor_tensor(
                        out=f_cv[:, 1, 0:1], in0=f_tail[:, m:m + 1],
                        scalar=c_conv[:, m * 4:m * 4 + 1],
                        in1=f_cv[:, 0, 0:1], op0=ALU.mult, op1=ALU.add)],
              reads=["f_tail", "c_conv", "f_cv0"], writes=["f_cv1"])
            E("dve", [lambda e, m=m, hb=hb, ntok=ntok, nconv=nconv:
                      e.scalar_tensor_tensor(
                        out=f_cv[:, 0, 0:ntok - 1], in0=PB[hb][:, 1:ntok],
                        scalar=c_conv[:, m * 4 + 2:m * 4 + 3],
                        in1=f_cv[:, 1, 0:ntok - 1], op0=ALU.mult, op1=ALU.add)],
              reads=[f"P{hb}", "c_conv", "f_cv1"], writes=["f_cv0"])
            if halo_slot is not None:
                E("dve", [lambda e, m=m, hb=hb, ntok=ntok:
                          e.scalar_tensor_tensor(
                            out=f_cv[:, 0, ntok - 1:ntok],
                            in0=PB[hb][:, ntok:ntok + 1],
                            scalar=c_conv[:, m * 4 + 2:m * 4 + 3],
                            in1=f_cv[:, 1, ntok - 1:ntok],
                            op0=ALU.mult, op1=ALU.add)],
                  reads=[f"P{hb}", "c_conv", "f_cv1"], writes=["f_cv0"])
            else:
                E("dve", [lambda e, ntok=ntok: e.tensor_copy(
                            f_cv[:, 0, ntok - 1:ntok],
                            f_cv[:, 1, ntok - 1:ntok])],
                  reads=["f_cv1"], writes=["f_cv0"])
            E("dve", [lambda e, m=m, hb=hb, ntok=ntok: e.tensor_copy(
                        f_tail[:, m:m + 1], PB[hb][:, ntok - 1:ntok])],
              reads=[f"P{hb}"], writes=["f_tail"])
            E("act", [lambda e, m=m, ntok=ntok: e.activation(
                        out=f_h4[:, m, 0:ntok], in_=f_cv[:, 0, 0:ntok],
                        func=AF.Silu)],
              reads=["f_cv0"], writes=[f"h4.{m}", "h4x"])

        E("dve", [lambda e, ntok=ntok: e.tensor_reduce(
                    out=f_M[:, 0:ntok],
                    in_=f_h4[:][:, :, 0:ntok].transpose([0, 2, 1]),
                    axis=AX.X, op=ALU.max, apply_absolute_value=True)],
          reads=[f"h4x"], writes=["f_M"])
        for b in range(nch):
            E("pe", [lambda e, b=b: e.transpose(
                        PB[7][:, 0:128],
                        f_M[:, b * 128:(b + 1) * 128], c_identf[:])],
              reads=["f_M", "c_identf"], writes=["P7"])
            E("dve", [lambda e, b=b: e.tensor_reduce(
                        out=f_tok[:, 4 + b:5 + b], in_=PB[7][:, 0:128],
                        axis=AX.X, op=ALU.max)],
              reads=["P7"], writes=["f_tokM"])
            E("dve", [lambda e, b=b: e.tensor_scalar(
                        out=f_tok[:, 4 + b:5 + b], in0=f_tok[:, 4 + b:5 + b],
                        scalar1=1e-5, scalar2=None, op0=ALU.max)],
              reads=["f_tokM"], writes=["f_tokM"])
            E("dve", [lambda e, b=b: e.reciprocal(
                        out=f_tok[:, 2 + b:3 + b], in_=f_tok[:, 4 + b:5 + b])],
              reads=["f_tokM"], writes=["f_tokS"])
            E("dve", [lambda e, b=b: e.tensor_scalar(
                        out=f_tok[:, 2 + b:3 + b], in0=f_tok[:, 2 + b:3 + b],
                        scalar1=127.0, scalar2=None, op0=ALU.mult)],
              reads=["f_tokS"], writes=["f_tokS"])
            E("dve", [lambda e, b=b: e.tensor_scalar(
                        out=f_tok[:, b:b + 1], in0=f_tok[:, 4 + b:5 + b],
                        scalar1=c_sw[:, 1:2], scalar2=1.0 / 127.0,
                        op0=ALU.mult, op1=ALU.mult)],
              reads=["f_tokM", "c_sw1"], writes=["f_tok"])
            E("pe", [lambda e, b=b: e.matmul(
                        PB[7][0:1, 0:128], f_tok[:, 2 + b:3 + b], c_identf[:],
                        start=True, stop=True)],
              reads=["f_tokS", "c_identf"], writes=["P7"])
            E("dve", [lambda e, b=b: e.tensor_copy(
                        f_row[0:1, b * 128:(b + 1) * 128], PB[7][0:1, 0:128])],
              reads=["P7"], writes=["f_rowS"])
        E("pe", [lambda e, ntok=ntok: e.matmul(
                    PB[7][:, 0:ntok], c_onesf[0:1, :], f_row[0:1, 0:ntok],
                    start=True, stop=True)],
          reads=["f_rowS", "c_onesf"], writes=["P7"])
        E("dve", [lambda e, ntok=ntok: e.tensor_copy(
                    f_s2[:, 0:ntok], PB[7][:, 0:ntok])],
          reads=["P7"], writes=["f_s2"])

        for m in range(32):
            xsl = m % 2
            E("dve", [lambda e, m=m, ntok=ntok, xsl=xsl: e.tensor_tensor(
                        out=f_qs[:, xsl, 0:ntok], in0=f_h4[:, m, 0:ntok],
                        in1=f_s2[:, 0:ntok], op=ALU.mult)],
              reads=[f"h4.{m}", "f_s2"], writes=[f"f_qs{xsl}"])
            E("act", [lambda e, ntok=ntok, xsl=xsl: e.activation(
                        out=f_qs[:, xsl, 0:ntok], in_=f_qs[:, xsl, 0:ntok],
                        func=AF.Identity, bias=c_b[:, 4:5], scale=1.0)],
              reads=[f"f_qs{xsl}", "c_b"], writes=[f"f_qs{xsl}"])
            E("act", [lambda e, ntok=ntok, xsl=xsl: e.copy(
                        f_qu[:, xsl, 0:ntok], f_qs[:, xsl, 0:ntok])],
              reads=[f"f_qs{xsl}"], writes=[f"f_qu{xsl}"])
            E("dve", [lambda e, ntok=ntok, xsl=xsl: e.tensor_tensor(
                        out=f_qd[:, xsl, 0:ntok], in0=f_qu[:, xsl, 0:ntok],
                        in1=f_qs[:, xsl, 0:ntok], op=ALU.is_gt)],
              reads=[f"f_qu{xsl}", f"f_qs{xsl}"], writes=[f"f_qd{xsl}"])
            E("dve", [lambda e, xsl=xsl, ntok=ntok: e.scalar_tensor_tensor(
                        out=f_xq2[:, xsl, 0:ntok], in0=f_qu[:, xsl, 0:ntok],
                        scalar=-128.0, in1=f_qd[:, xsl, 0:ntok],
                        op0=ALU.add, op1=ALU.subtract)],
              reads=[f"f_qu{xsl}", f"f_qd{xsl}"], writes=[f"xq2.{xsl}"])
            for s in range(nch):
                bank = 3 + 2 * s
                fns = []
                fns.append(lambda e, m=m, xsl=xsl, s=s: e.ldweights(
                    f_xq2[:, xsl, s * 128:(s + 1) * 128]))
                for n in range(2):
                    fns.append(lambda e, m=m, xsl=xsl, s=s, n=n, bank=bank:
                               e.matmul(
                                   PB[bank + n][:],
                                   f_xq2[:, xsl, s * 128:(s + 1) * 128],
                                   f_w2[:, m, n * 512:(n + 1) * 512],
                                   start=(m == 0), stop=(m == 31)))
                E("pe", fns,
                  reads=[f"xq2.{xsl}", f"fw2{m}"],
                  writes=[f"P{bank}", f"P{bank + 1}"])
        if DBG and ft == 0:
            dump(8, ap_f32=f_x1[:, 0, :], rtag="fx1.0")
            dump(9, ap_bf16=f_xq1T[:, 0:4, 0:2, :], rtag=["xq1T.0", "xq1T.1"])
            dump(10, ap_f32=f_h4[:, 0, 0:256], rtag="h4.0")
            dump(11, ap_f32=f_h4[:, 17, 0:256], rtag="h4.17")
            dump(12, ap_f32=f_M[:, 0:256], rtag="f_M")
            dump(13, ap_f32=f_s2[:, 0:256], rtag="f_s2")
            dump(14, ap_f32=f_tok[:, 0:6], rtag=["f_tok", "f_tokS", "f_tokM"])
        for s in range(nch):
            bank = 3 + 2 * s
            xsl = (c0 + s) % 4
            for n in range(2):
                E("dve", [lambda e, s=s, n=n, bank=bank: e.tensor_scalar(
                            out=f_y2[:, n * 512:(n + 1) * 512],
                            in0=PB[bank + n][:],
                            scalar1=f_tok[:, s:s + 1], scalar2=None,
                            op0=ALU.mult)],
                  reads=[f"P{bank + n}", "f_tok"], writes=["y2"])
            if DBG and ft == 0 and s == 0:
                dump(15, ap_f32=f_y2[:], rtag="y2")
            fln_stats(f_y2[:, 0:D], ["y2"], 2, 7)
            E("dve", [lambda e: e.tensor_scalar(
                        out=f_y2[:], in0=f_y2[:],
                        scalar1=f_st[:, 2:3], scalar2=f_st[:, 7:8],
                        op0=ALU.subtract, op1=ALU.mult)],
              reads=["y2", "fstm", "fstr"], writes=["y2"])
            E("dve", [lambda e, xsl=xsl: e.tensor_tensor(
                        out=f_x1[:, xsl, :], in0=f_x1[:, xsl, :],
                        in1=f_y2[:], op=ALU.add)],
              reads=["y2", f"fx1.{xsl}"], writes=[f"fx1.{xsl}"])
            E("sp", [lambda e, s=s, c0=c0, xsl=xsl: e.dma_start(
                        out=y_t[:, c0 + s, :], in_=f_x1[:, xsl, :])],
              reads=[f"fx1.{xsl}"], writes=["y_hbm"], dma=True)

    prep_chunk(0)
    if NM > 1:
        prep_chunk(1)
    if _os.environ.get("SKIPF") == "1":
        n_tiles = 1
    for ft in range(n_tiles):
        nxt = ft * 2 + 2
        if nxt < NM:
            prep_chunk(nxt)
        if nxt + 1 < NM:
            prep_chunk(nxt + 1)
        ffn_tile(ft)

    tk.barrier()
    tk.close()
    st.close()
    return nc


def _host_inputs(inputs, NM, NP, core):
    b, j = core // 2, core % 2
    x = inputs["L0_attn_x"]
    TM, TP = NM * CH, NP * CH
    if j == 0:
        xm = x[b, 0:TM, :]
    else:
        xm = x[b, x.shape[1] - TM:, :]
    xp = x[b, 0:max(TP, 1), :]
    wqkv = np.concatenate([np.asarray(inputs["L0_attn_Wq"]),
                           np.asarray(inputs["L0_attn_Wk"]),
                           np.asarray(inputs["L0_attn_Wv"])], axis=1)
    wf = np.asarray(inputs["L0_attn_W_feat"])
    wfeat2 = np.concatenate([wf, wf], axis=0)
    cw = np.asarray(inputs["L0_ffn_conv_w"]).reshape(DH, 3)
    cb = np.asarray(inputs["L0_ffn_conv_b"]).reshape(DH)
    convpk = np.zeros((128, 128), dtype=np.float32)
    for m in range(32):
        convpk[:, m * 4 + 0] = cw[m * 128:(m + 1) * 128, 0]
        convpk[:, m * 4 + 1] = cw[m * 128:(m + 1) * 128, 1]
        convpk[:, m * 4 + 2] = cw[m * 128:(m + 1) * 128, 2]
        convpk[:, m * 4 + 3] = cb[m * 128:(m + 1) * 128]
    import ml_dtypes
    ident = np.eye(128, dtype=ml_dtypes.bfloat16)
    maskT = np.triu(np.ones((128, 128), dtype=ml_dtypes.bfloat16))
    pscale = np.full((128, 1), 1.0 if j == 1 else 0.0, dtype=np.float32)
    biasc = np.zeros((128, 8), dtype=np.float32)
    biasc[:, 0] = EPS_LN
    biasc[:, 1] = K0 - LSQ
    biasc[:, 2] = -K0 + math.log(EPS_ATT) + LSQ
    biasc[:, 3] = -K0
    biasc[:, 4] = MAGIC
    return {
        "x_main": np.ascontiguousarray(xm, dtype=np.float32),
        "x_pref": np.ascontiguousarray(xp, dtype=np.float32),
        "wqkv": np.ascontiguousarray(wqkv, dtype=np.float32),
        "wo": np.ascontiguousarray(inputs["L0_attn_Wo"], dtype=np.float32),
        "wfeat2": np.ascontiguousarray(wfeat2, dtype=np.float32),
        "w1T": np.ascontiguousarray(np.asarray(inputs["L0_ffn_w1"]).T,
                                    dtype=np.float32),
        "w2T": np.ascontiguousarray(np.asarray(inputs["L0_ffn_w2"]).T,
                                    dtype=np.float32),
        "convpk": convpk,
        "ident": ident,
        "identf": np.eye(128, dtype=np.float32),
        "onesf": np.ones((128, 128), dtype=np.float32),
        "maskT": maskT,
        "pscale": pscale,
        "biasc": biasc,
    }


def kernel(**inputs):
    NM, NP = FULL["NM"], FULL["NP"]
    nc = bass.Bass("TRN2", target_bir_lowering=False)
    build_program(nc, NM, NP)
    in_maps = [_host_inputs(inputs, NM, NP, c) for c in range(8)]
    res = run_bass_kernel_spmd(nc, in_maps, core_ids=list(range(8)))
    T = 8192
    out = np.zeros((4, T, D), dtype=np.float32)
    for c in range(8):
        b, j = c // 2, c % 2
        yc = res.results[c]["y"]
        yc = yc.reshape(NM * CH, D)
        if j == 0:
            out[b, 0:T0, :] = yc[0:T0, :]
        else:
            out[b, T0:, :] = yc[NM * CH - (T - T0):, :]
    return out


# version 09 (final)
# speedup vs baseline: 1.0649x; 1.0649x over previous
import math
import numpy as np
import concourse.bass as bass
import concourse.mybir as mybir
from concourse.bass_utils import run_bass_kernel_spmd
import bass_rust

F32 = mybir.dt.float32
BF16 = mybir.dt.bfloat16
U8 = mybir.dt.uint8
AF = mybir.ActivationFunctionType
ALU = mybir.AluOpType
AX = mybir.AxisListType

D = 1024
DH = 4096
H = 16
HD = 64
NF = 128
CH = 128
K0 = 14.0
LSQ = math.log(math.sqrt(NF))
EPS_ATT = 1e-6
EPS_LN = 1e-5
MAGIC = 128.5

FULL = dict(NM=33, NP=31)
T0 = 4096


class TK:
    CSEM = {"sp": None, "pe": None, "dve": None, "act": None, "pool": None}

    DLANES = {"sp": 8, "pool": 4}

    def __init__(self, nc):
        self.nc = nc
        self.eng = {
            "sp": nc.sync, "pe": nc.tensor, "dve": nc.vector,
            "act": nc.scalar, "pool": nc.gpsimd,
        }
        self._cm = []
        self.sem = {}
        names = ["sp", "pe", "dve", "act", "pool"]
        names += [f"dsp{i}" for i in range(self.DLANES["sp"])]
        names += [f"dpl{i}" for i in range(self.DLANES["pool"])]
        for s in names:
            cm = nc.semaphore("sem_" + s)
            h = cm.__enter__()
            self._cm.append(cm)
            self.sem[s] = h
        self.cnt = {s: 0 for s in self.sem}
        self.seen = {e: {s: 0 for s in self.sem} for e in self.eng}
        self.writer = {}
        self.readers = {}
        self.dr = {e: 0 for e in self.eng}
        self.dlane = {"sp": 0, "pool": 0}
        self.bounds = {e: {0} for e in self.eng}

    def close(self):
        for cm in reversed(self._cm):
            cm.__exit__(None, None, None)


    def _wait(self, e, s, v):
        self.eng[e].wait_ge(self.sem[s], v)
        self.seen[e][s] = max(self.seen[e][s], v)
        self.bounds[e].add(self.cnt[e])

    def _need(self, e, tok, own_dma_sem):
        s, v = tok
        if s == e:
            if e != "pe" and self.dr[e] < v:
                self.eng[e].drain()
                self.dr[e] = self.cnt[e]
            return
        if s in self.eng:
            b = min((x for x in self.bounds[s] if x >= v), default=None)
            if b is None:
                self._wait(s, s, self.cnt[s])
                b = self.cnt[s]
            v = b
        if self.seen[e][s] >= v:
            return
        self._wait(e, s, v)

    LIMIT = None
    NEMIT = 0

    def emit(self, e, fns, reads=(), writes=(), dma=False, label=None):
        TK.NEMIT += 1
        if TK.LIMIT is not None and TK.NEMIT > TK.LIMIT:
            return ("sp", 0)
        deps = []
        for b in reads:
            t = self.writer.get(b)
            if t is not None:
                deps.append(t)
        for b in writes:
            t = self.writer.get(b)
            if t is not None:
                deps.append(t)
            deps.extend(self.readers.get(b, ()))
        best = {}
        for s, v in deps:
            if best.get(s, -1) < v:
                best[s] = v
        for s, v in best.items():
            self._need(e, (s, v), None)
        if dma:
            assert len(fns) == 1, "dma groups must be single-instruction"
            q = "sp" if e == "sp" else "pool"
            pfx = "dsp" if e == "sp" else "dpl"
            lane = self.dlane[q]
            self.dlane[q] = (lane + 1) % self.DLANES[q]
            own = f"{pfx}{lane}"
            if self.seen[e][own] < self.cnt[own]:
                self.eng[e].wait_ge(self.sem[own], self.cnt[own])
                self.seen[e][own] = self.cnt[own]
            ins = fns[0](self.eng[e])
            ins.then_inc(self.sem[own], 16)
            self.cnt[own] += 16
            tok = (own, self.cnt[own])
        else:
            own = e
            h = self.sem[own]
            instrs = []
            for f in fns:
                instrs.append(f(self.eng[e]))
            instrs[-1].then_inc(h, 1)
            self.cnt[own] += 1
            tok = (own, self.cnt[own])
        for b in writes:
            self.writer[b] = tok
            self.readers[b] = []
        for b in reads:
            self.readers.setdefault(b, []).append(tok)
        return tok

    def barrier(self):
        for e in self.eng:
            own = e
            self.eng[e].sem_inc(self.sem[own], 1)
            self.cnt[own] += 1
        for e in self.eng:
            if e in self.bounds:
                self.bounds[e].add(self.cnt[e])
        for e in self.eng:
            for s in self.sem:
                if self.cnt[s] > self.seen[e][s]:
                    self._wait(e, s, self.cnt[s])
            self.dr[e] = self.cnt[e]
        self.writer.clear()
        self.readers.clear()


def build_program(nc, NM, NP):
    TM = NM * CH
    TP = NP * CH
    DT = nc.dram_tensor

    x_main = DT("x_main", [TM, D], F32, kind="ExternalInput").ap()
    x_pref = DT("x_pref", [max(TP, 1), D], F32, kind="ExternalInput").ap()
    wqkv_h = DT("wqkv", [D, 3 * D], F32, kind="ExternalInput").ap()
    wo_h = DT("wo", [D, D], F32, kind="ExternalInput").ap()
    wf_h = DT("wfeat2", [128, NF], F32, kind="ExternalInput").ap()
    w1T_h = DT("w1T", [D, DH], F32, kind="ExternalInput").ap()
    w2T_h = DT("w2T", [DH, D], F32, kind="ExternalInput").ap()
    convpk_h = DT("convpk", [128, 128], F32, kind="ExternalInput").ap()
    ident_h = DT("ident", [128, 128], BF16, kind="ExternalInput").ap()
    identf_h = DT("identf", [128, 128], F32, kind="ExternalInput").ap()
    onesf_h = DT("onesf", [128, 128], F32, kind="ExternalInput").ap()
    maskT_h = DT("maskT", [128, 128], BF16, kind="ExternalInput").ap()
    pscale_h = DT("pscale", [128, 1], F32, kind="ExternalInput").ap()
    biasc_h = DT("biasc", [128, 8], F32, kind="ExternalInput").ap()
    y_out = DT("y", [TM, D], F32, kind="ExternalOutput").ap()
    import os as _os
    DBG = _os.environ.get("BASSDBG") == "1"
    dbg = DT("dbg", [16, 128, 2048], F32, kind="ExternalOutput").ap() if DBG else None
    dbg_state = {"used": False}

    def dump(idx, ap_bf16=None, ap_f32=None, rtag=None):
        if not DBG:
            return
        rtags = rtag if isinstance(rtag, list) else [rtag]
        _shp = (ap_bf16 if ap_bf16 is not None else ap_f32).shape
        n = 1
        for _d in _shp[1:]:
            n *= _d
        if ap_bf16 is not None:
            E("pool", [lambda e: e.dma_start(out=dbg[idx, :, 0:n],
                                             in_=ap_bf16)],
              reads=rtags, writes=["dbg_hbm"], dma=True)
        else:
            E("sp", [lambda e: e.dma_start(out=dbg[idx, :, 0:n], in_=ap_f32)],
              reads=rtags, writes=["dbg_hbm"], dma=True)
    x1s = DT("x1s", [TM, D], F32, kind="Internal").ap()
    w1qs = DT("w1qs", [D, DH], BF16, kind="Internal").ap()
    w2qs = DT("w2qs", [DH, D], BF16, kind="Internal").ap()

    xm_t = x_main.rearrange("(c p) d -> p c d", p=CH)
    xp_t = x_pref.rearrange("(c p) d -> p c d", p=CH) if NP else None
    x1s_t = x1s.rearrange("(c p) d -> p c d", p=CH)
    y_t = y_out.rearrange("(c p) d -> p c d", p=CH)

    tk = TK(nc)
    E = tk.emit

    import contextlib
    st = contextlib.ExitStack()
    SB = lambda name, shape, dt: st.enter_context(nc.sbuf_tensor(name, shape, dt))
    PS = lambda name, shape, dt: st.enter_context(nc.psum_tensor(name, shape, dt))

    c_ident = SB("c_ident", [128, 128], BF16)
    c_identf = SB("c_identf", [128, 128], F32)
    c_onesf = SB("c_onesf", [128, 128], F32)
    c_mask = SB("c_mask", [128, 128], BF16)
    c_wf = SB("c_wf", [128, NF], BF16)
    c_conv = SB("c_conv", [128, 128], F32)
    c_ps = SB("c_ps", [128, 1], F32)
    c_b = SB("c_b", [128, 8], F32)
    c_sw = SB("c_sw", [128, 2], F32)

    P = [PS(f"pp{i}", [128, 512], F32) for i in range(8)]
    PB = [p[:] for p in P]
    PBb = [p[:].bitcast(BF16) for p in P]

    for fn, w in [
            (lambda e: e.dma_start(out=c_ident[:], in_=ident_h[:]), "c_ident"),
            (lambda e: e.dma_start(out=c_identf[:], in_=identf_h[:]), "c_identf"),
            (lambda e: e.dma_start(out=c_onesf[:], in_=onesf_h[:]), "c_onesf"),
            (lambda e: e.dma_start(out=c_mask[:], in_=maskT_h[:]), "c_mask"),
            (lambda e: e.dma_start(out=c_conv[:], in_=convpk_h[:]), "c_conv"),
            (lambda e: e.dma_start(out=c_ps[:], in_=pscale_h[:]), "c_ps"),
            (lambda e: e.dma_start(out=c_b[:], in_=biasc_h[:]), "c_b")]:
        E("sp", [fn], writes=[w], dma=True)
    E("pool", [lambda e: e.dma_start(out=c_wf[:], in_=wf_h[:])],
      writes=["c_wf"], dma=True)
    E("act", [lambda e: e.preload_act_table(AF.Exp)], writes=["acttab"])


    stA = contextlib.ExitStack()
    SA = lambda name, shape, dt: stA.enter_context(nc.sbuf_tensor(name, shape, dt))
    a_S = SA("a_S", [128, H * 65], F32)
    a_Sb = SA("a_Sb", [128, H * 65], BF16)
    a_wqkv = SA("a_wqkv", [128, 8, 3 * D], BF16)
    a_wo = SA("a_wo", [128, 8, D], BF16)
    a_x0 = SA("a_x0", [128, 2, D], F32)
    a_xn = SA("a_xn", [128, 2, D], BF16)
    a_xnT = SA("a_xnT", [128, 2, 8, 128], BF16)
    a_q = SA("a_q", [128, 2, D], BF16)
    a_k = SA("a_k", [128, 2, D], BF16)
    a_v = SA("a_v", [128, 2, D], BF16)
    a_qT = SA("a_qT", [128, 2, 8, 128], BF16)
    a_kT = SA("a_kT", [128, 2, 8, 128], BF16)
    a_sq = SA("a_sq", [128, D], F32)
    a_ao = SA("a_ao", [128, D], F32)
    a_st = SA("a_st", [128, 48], F32)
    a_sqq = SA("a_sqq", [128, 2, H], F32)
    a_sqk = SA("a_sqk", [128, 2, H], F32)
    a_ct = SA("a_ct", [128, 2, H], F32)
    a_eps = SA("a_eps", [128, 2, H], F32)
    a_den = SA("a_den", [128, H], F32)
    a_qpT = SA("a_qpT", [128, H, 128], BF16)
    a_kpT = SA("a_kpT", [128, H, 128], BF16)
    a_kp = SA("a_kp", [128, H, 128], BF16)
    a_vp = SA("a_vp", [128, 2, H, 65], BF16)
    a_atm = SA("a_atm", [128, 2, 4, 128], BF16)
    a_y = SA("a_y", [128, D], BF16)
    a_yT = SA("a_yT", [128, 8, 128], BF16)
    a_x1 = SA("a_x1", [128, 2, D], F32)

    for k in range(8):
        E("pool", [lambda e, k=k: e.dma_start(
                    out=a_wqkv[:, k, :], in_=wqkv_h[k * 128:(k + 1) * 128, :])],
          writes=[f"wqkv{k}"], dma=True)
        E("pool", [lambda e, k=k: e.dma_start(
                    out=a_wo[:, k, :], in_=wo_h[k * 128:(k + 1) * 128, :])],
          writes=[f"wo{k}"], dma=True)
    E("dve", [lambda e: e.memset(a_S[:], 0.0)], writes=["a_S"])
    E("dve", [lambda e: e.memset(a_Sb[:], 0.0)], writes=["a_Sb"])

    q_in = SA("q_in", [128, 2, DH], F32)
    q_t = SA("q_t", [128, DH // 2], F32)
    q_u = SA("q_u", [128, DH // 2], U8)
    q_d = SA("q_d", [128, DH // 2], U8)
    q_bf = SA("q_bf", [128, DH // 2], BF16)
    q_acc = SA("q_acc", [128, 2], F32)
    q_s = SA("q_s", [128, 2], F32)
    qsteps = []

    def _q_mean(wi, w_h, n_r, n_c):
        nch = n_r // 128
        yield lambda: E("dve", [lambda e: e.memset(q_acc[:, wi:wi + 1], 0.0)],
                        writes=[f"q_acc{wi}"])
        for ci in range(nch):
            def s(ci=ci, sl=ci % 2):
                E("sp", [lambda e: e.dma_start(out=q_in[:, sl, 0:n_c],
                                               in_=w_h[ci * 128:(ci + 1) * 128, :])],
                  writes=[f"q_in{sl}"], dma=True)
                E("dve", [lambda e: e.reduce_sum(out=q_t[:, 0:1],
                                                 in_=q_in[:, sl, 0:n_c],
                                                 axis=AX.X,
                                                 apply_absolute_value=True)],
                  reads=[f"q_in{sl}"], writes=["q_t"])
                E("dve", [lambda e: e.tensor_tensor(out=q_acc[:, wi:wi + 1],
                                                    in0=q_acc[:, wi:wi + 1],
                                                    in1=q_t[:, 0:1],
                                                    op=ALU.add)],
                  reads=["q_t", f"q_acc{wi}"], writes=[f"q_acc{wi}"])
            yield s
        def fin():
            E("pe", [lambda e: e.matmul(
                        PB[7][0:1, 0:1], c_onesf[:, 0:1], q_acc[:, wi:wi + 1],
                        start=True, stop=True)],
              reads=[f"q_acc{wi}", "c_onesf"], writes=["P7"])
            E("dve", [lambda e: e.tensor_copy(q_s[0:1, wi:wi + 1],
                                              PB[7][0:1, 0:1])],
              reads=["P7"], writes=[f"q_sr{wi}"])
            E("pe", [lambda e: e.matmul(
                        PB[7][:, 0:1], c_onesf[0:1, :], q_s[0:1, wi:wi + 1],
                        start=True, stop=True)],
              reads=[f"q_sr{wi}", "c_onesf"], writes=["P7"])
            E("dve", [lambda e: e.tensor_copy(q_s[:, wi:wi + 1],
                                              PB[7][:, 0:1])],
              reads=["P7"], writes=[f"q_s{wi}"])
            E("dve", [lambda e, n=(n_r * n_c): e.tensor_scalar(
                        out=c_sw[:, wi:wi + 1], in0=q_s[:, wi:wi + 1],
                        scalar1=1.0 / n, scalar2=1e-5,
                        op0=ALU.mult, op1=ALU.max)],
              reads=[f"q_s{wi}"], writes=[f"c_sw{wi}"])
            E("dve", [lambda e: e.reciprocal(out=q_s[:, wi:wi + 1],
                                             in_=c_sw[:, wi:wi + 1])],
              reads=[f"c_sw{wi}"], writes=[f"q_s{wi}"])
        yield fin

    def _q_quant(wi, w_h, wq_h, n_r, n_c):
        nch = n_r // 128
        for ci in range(nch):
            def s(ci=ci, sl=ci % 2):
                E("sp", [lambda e: e.dma_start(out=q_in[:, sl, 0:n_c],
                                               in_=w_h[ci * 128:(ci + 1) * 128, :])],
                  writes=[f"q_in{sl}"], dma=True)
                for hf in range(0, n_c, DH // 2):
                    nn = min(DH // 2, n_c - hf)
                    E("act", [lambda e, hf=hf, nn=nn, sl=sl: e.activation(
                                out=q_t[:, 0:nn], in_=q_in[:, sl, hf:hf + nn],
                                func=AF.Identity, bias=c_b[:, 4:5],
                                scale=q_s[:, wi:wi + 1])],
                      reads=[f"q_in{sl}", f"q_s{wi}", "c_b"], writes=["q_t"])
                    E("act", [lambda e, nn=nn: e.copy(q_u[:, 0:nn],
                                                      q_t[:, 0:nn])],
                      reads=["q_t"], writes=["q_u"])
                    E("dve", [lambda e, nn=nn: e.tensor_tensor(
                                out=q_d[:, 0:nn], in0=q_u[:, 0:nn],
                                in1=q_t[:, 0:nn], op=ALU.is_gt)],
                      reads=["q_u", "q_t"], writes=["q_d"])
                    E("dve", [lambda e, nn=nn: e.tensor_tensor(
                                out=q_t[:, 0:nn], in0=q_u[:, 0:nn],
                                in1=q_d[:, 0:nn], op=ALU.subtract)],
                      reads=["q_u", "q_d"], writes=["q_t"])
                    E("dve", [lambda e, nn=nn: e.tensor_scalar(
                                out=q_t[:, 0:nn], in0=q_t[:, 0:nn],
                                scalar1=-128.0, scalar2=-1.0,
                                op0=ALU.add, op1=ALU.max)],
                      reads=["q_t"], writes=["q_t"])
                    E("dve", [lambda e, nn=nn: e.tensor_scalar(
                                out=q_bf[:, 0:nn], in0=q_t[:, 0:nn],
                                scalar1=1.0, scalar2=None, op0=ALU.min)],
                      reads=["q_t"], writes=["q_bf"])
                    E("sp", [lambda e, ci=ci, hf=hf, nn=nn: e.dma_start(
                                out=wq_h[ci * 128:(ci + 1) * 128, hf:hf + nn],
                                in_=q_bf[:, 0:nn])],
                      reads=["q_bf"], writes=["wq_hbm"], dma=True)
            yield s

    for g in (_q_mean(0, w1T_h, D, DH), _q_mean(1, w2T_h, DH, D)):
        qsteps.extend(g)
    for g in (_q_quant(0, w1T_h, w1qs, D, DH), _q_quant(1, w2T_h, w2qs, DH, D)):
        qsteps.extend(g)
    qsteps.reverse()

    def ln_stats(xin_ap, rbufs, cb):
        def C(i):
            return a_st[:, cb + i:cb + i + 1]
        E("dve", [lambda e: e.bn_stats(a_st[:, cb + 8:cb + 14],
                                       xin_ap[:, 0:512])],
          reads=rbufs, writes=[f"st{cb+8}"])
        E("dve", [lambda e: e.bn_stats(a_st[:, cb + 14:cb + 20],
                                       xin_ap[:, 512:1024])],
          reads=rbufs, writes=[f"st{cb+14}"])
        E("dve", [lambda e: e.bn_aggr(a_st[:, cb + 2:cb + 4],
                                      a_st[:, cb + 8:cb + 20])],
          reads=[f"st{cb+8}", f"st{cb+14}"], writes=[f"st{cb+2}", f"st{cb+3}"])
        E("dve", [lambda e: e.tensor_scalar(out=C(6), in0=C(3),
                                            scalar1=EPS_LN, scalar2=None,
                                            op0=ALU.add)],
          reads=[f"st{cb+3}"], writes=[f"st{cb+6}"])
        E("dve", [lambda e: e.tensor_scalar(
                    out=C(4).bitcast(mybir.dt.int32)[:, 0:1],
                    in0=C(6).bitcast(mybir.dt.int32)[:, 0:1],
                    scalar1=1, scalar2=-1, op0=ALU.logical_shift_right,
                    op1=ALU.bitwise_xor)],
          reads=[f"st{cb+6}"], writes=[f"st{cb+4}"])
        E("dve", [lambda e: e.tensor_scalar(
                    out=C(4).bitcast(mybir.dt.int32)[:, 0:1],
                    in0=C(4).bitcast(mybir.dt.int32)[:, 0:1],
                    scalar1=0x5f3759e0, scalar2=None, op0=ALU.add)],
          reads=[f"st{cb+4}"], writes=[f"st{cb+4}"])
        for _ in range(3):
            E("dve", [lambda e: e.tensor_tensor(out=C(7), in0=C(4), in1=C(4),
                                                op=ALU.mult)],
              reads=[f"st{cb+4}"], writes=[f"st{cb+7}"])
            E("dve", [lambda e: e.tensor_tensor(out=C(7), in0=C(7), in1=C(6),
                                                op=ALU.mult)],
              reads=[f"st{cb+7}", f"st{cb+6}"], writes=[f"st{cb+7}"])
            E("dve", [lambda e: e.tensor_scalar(out=C(7), in0=C(7),
                                                scalar1=-0.5, scalar2=1.5,
                                                op0=ALU.mult, op1=ALU.add)],
              reads=[f"st{cb+7}"], writes=[f"st{cb+7}"])
            E("dve", [lambda e: e.tensor_tensor(out=C(4), in0=C(4), in1=C(7),
                                                op=ALU.mult)],
              reads=[f"st{cb+4}", f"st{cb+7}"], writes=[f"st{cb+4}"])

    def transpose8(src_ap, src_tag, dst_ap, dst_tag, bank):
        fns = []
        for k in range(8):
            fns.append(lambda e, k=k: e.transpose(
                PBb[bank][:, k * 128:(k + 1) * 128],
                src_ap[:, k * 128:(k + 1) * 128], c_ident[:]))
        E("pe", fns, reads=[src_tag, "c_ident"], writes=[f"P{bank}"])
        E("act", [lambda e: e.copy(
                    dst_ap.rearrange("p a b -> p (a b)"), PBb[bank][:])],
          reads=[f"P{bank}"], writes=[dst_tag])

    def front(cidx, is_prefix, x_t, sl):
        xt = f"x0.{sl}"
        E("sp", [lambda e: e.dma_start(out=a_x0[:, sl, :],
                                       in_=x_t[:, cidx, :])],
          writes=[xt], dma=True)
        ln_stats(a_x0[:, sl, :], [xt], 0)
        E("dve", [lambda e: e.tensor_scalar(
                    out=a_xn[:, sl, :], in0=a_x0[:, sl, :],
                    scalar1=a_st[:, 2:3], scalar2=a_st[:, 4:5],
                    op0=ALU.subtract, op1=ALU.mult)],
          reads=[xt, "st2", "st4"], writes=[f"a_xn{sl}"])
        transpose8(a_xn[:, sl, :], f"a_xn{sl}", a_xnT[:, sl], f"a_xnT{sl}", 2)

        groups = [(0, a_q), (1, a_k), (2, a_v)]
        names = ["a_q", "a_k", "a_v"]
        if is_prefix:
            groups = groups[1:]
        for g, qsb in groups:
            qtag = f"{names[g]}{sl}"
            for half in range(2):
                bank = half
                fns = []
                for k in range(8):
                    fns.append(lambda e, k=k, sl=sl: e.ldweights(
                        a_xnT[:, sl, k, :]))
                    fns.append(lambda e, k=k, g=g, half=half, bank=bank, sl=sl:
                               e.matmul(PB[bank][:],
                                        a_xnT[:, sl, k, :],
                                        a_wqkv[:, k, g * D + half * 512:
                                               g * D + half * 512 + 512],
                                        start=(k == 0), stop=(k == 7)))
                E("pe", fns,
                  reads=[f"a_xnT{sl}"] + [f"wqkv{k}" for k in range(8)],
                  writes=[f"P{bank}"])
            if g in (0, 1):
                for half in range(2):
                    E("act", [lambda e, half=half: e.activation(
                                out=a_sq[:, half * 512:half * 512 + 512],
                                in_=PB[half][:], func=AF.Square)],
                      reads=[f"P{half}"], writes=["a_sq"])
                dst = a_sqq if g == 0 else a_sqk
                dtag = ("a_sqq" if g == 0 else "a_sqk") + str(sl)
                E("dve", [lambda e, dst=dst, sl=sl: e.tensor_reduce(
                            out=dst[:, sl], in_=a_sq[:].rearrange(
                                "p (h d) -> p h d", h=H),
                            axis=AX.X, op=ALU.add)],
                  reads=["a_sq"], writes=[dtag])
            for half in range(2):
                E("act", [lambda e, half=half, qsb=qsb, sl=sl: e.copy(
                            qsb[:, sl, half * 512:half * 512 + 512],
                            PB[half][:])],
                  reads=[f"P{half}"], writes=[qtag])

        if not is_prefix:
            transpose8(a_q[:, sl, :], f"a_q{sl}", a_qT[:, sl], f"a_qT{sl}", 2)
        transpose8(a_k[:, sl, :], f"a_k{sl}", a_kT[:, sl], f"a_kT{sl}", 2)

        E("act", [lambda e, sl=sl: e.activation(out=a_ct[:, sl],
                                                in_=a_sqk[:, sl],
                                                func=AF.Exp, bias=c_b[:, 1:2],
                                                scale=-0.5)],
          reads=[f"a_sqk{sl}", "c_b"], writes=[f"a_ct{sl}"])
        if not is_prefix:
            E("act", [lambda e, sl=sl: e.activation(out=a_eps[:, sl],
                                                    in_=a_sqq[:, sl],
                                                    func=AF.Exp,
                                                    bias=c_b[:, 2:3],
                                                    scale=0.5)],
              reads=[f"a_sqq{sl}", "c_b"], writes=[f"a_eps{sl}"])

        E("dve", [lambda e, sl=sl: e.tensor_tensor(
                    out=a_vp[:, sl, :, 0:64],
                    in0=a_v[:, sl, :].rearrange("p (h d) -> p h d", h=H),
                    in1=a_ct[:, sl].unsqueeze(2).broadcast_to([128, H, 64]),
                    op=ALU.mult)],
          reads=[f"a_v{sl}", f"a_ct{sl}"], writes=[f"a_vp{sl}"])
        E("dve", [lambda e, sl=sl: e.tensor_copy(a_vp[:, sl, :, 64:65],
                                                 a_ct[:, sl].unsqueeze(2))],
          reads=[f"a_ct{sl}"], writes=[f"a_vp{sl}"])

    HGRP = [[0, 2, 4, 6], [8, 10, 12, 14], [1, 3, 5, 7], [9, 11, 13, 15]]

    def featurize(dst_sb, dst_tag, src_sb, src_tag, tok_major):
        for heads in HGRP:
            bank = nb()
            po = (heads[0] % 2) * 64
            fns = []
            for hh, h in enumerate(heads):
                ko = h // 2
                if tok_major:
                    fns.append(lambda e, ko=ko, po=po: e.ldweights(
                        src_sb[po:po + 64, ko, :], tile_position=(po, 0)))
                    fns.append(lambda e, ko=ko, po=po, hh=hh: e.matmul(
                        PB[bank][:, hh * 128:hh * 128 + 128],
                        src_sb[po:po + 64, ko, :],
                        c_wf[po:po + 64, :],
                        start=(hh == 0), stop=(hh == 3)))
                else:
                    fns.append(lambda e, po=po: e.ldweights(
                        c_wf[po:po + 64, :], tile_position=(po, 0)))
                    fns.append(lambda e, ko=ko, po=po, hh=hh: e.matmul(
                        PB[bank][:, hh * 128:hh * 128 + 128],
                        c_wf[po:po + 64, :],
                        src_sb[po:po + 64, ko, :],
                        start=(hh == 0), stop=(hh == 3)))
            E("pe", fns, reads=[src_tag, "c_wf"], writes=[f"P{bank}"])
            E("act", [lambda e, heads=heads, bank=bank, dst_sb=dst_sb:
                      e.activation(
                        out=dst_sb[:, heads[0]:heads[3] + 1:2, :],
                        in_=PB[bank][:].rearrange("p (g f) -> p g f", g=4),
                        func=AF.Exp, bias=c_b[:, 3:4], scale=1.0)],
              reads=[f"P{bank}"], writes=[dst_tag])

    rot = {"i": 0}

    def nb():
        rot["i"] += 1
        return 3 + (rot["i"] % 5)

    def back(cidx, is_prefix, sl):
        featurize(a_kp, "a_kp", a_kT[:, sl], f"a_kT{sl}", True)
        if not is_prefix:
            featurize(a_kpT, "a_kpT", a_kT[:, sl], f"a_kT{sl}", False)
            featurize(a_qpT, "a_qpT", a_qT[:, sl], f"a_qT{sl}", False)

            for hg in range(4):
                bA = nb()
                bO = nb()
                fns = []
                for hh in range(4):
                    h = hg * 4 + hh
                    fns.append(lambda e, h=h: e.ldweights(a_kpT[:, h, :]))
                    fns.append(lambda e, h=h, hh=hh, bA=bA: e.matmul(
                        PB[bA][:, hh * 128:hh * 128 + 128],
                        a_kpT[:, h, :], a_qpT[:, h, :],
                        start=(hh == 0), stop=(hh == 3)))
                E("pe", fns, reads=["a_kpT", "a_qpT"], writes=[f"P{bA}"])
                E("dve", [lambda e, bA=bA, hg=hg: e.tensor_tensor(
                            out=a_atm[:, hg % 2],
                            in0=PB[bA][:].rearrange("p (g f) -> p g f", g=4),
                            in1=c_mask[:].unsqueeze(1).broadcast_to(
                                [128, 4, 128]),
                            op=ALU.mult)],
                  reads=[f"P{bA}", "c_mask"], writes=[f"a_atm{hg % 2}"])
                fns = []
                for hh in range(4):
                    h = hg * 4 + hh
                    fns.append(lambda e, hh=hh, hg=hg: e.ldweights(
                        a_atm[:, hg % 2, hh, :]))
                    fns.append(lambda e, h=h, hh=hh, sl=sl, bO=bO, hg=hg:
                               e.matmul(
                        PB[bO][:, hh * 65:hh * 65 + 65],
                        a_atm[:, hg % 2, hh, :], a_vp[:, sl, h, :],
                        start=(hh == 0), stop=False))
                    fns.append(lambda e, h=h: e.ldweights(a_qpT[:, h, :]))
                    fns.append(lambda e, h=h, hh=hh, bO=bO: e.matmul(
                        PB[bO][:, hh * 65:hh * 65 + 65],
                        a_qpT[:, h, :],
                        a_Sb[:, h * 65:h * 65 + 65],
                        start=False, stop=(hh == 3)))
                E("pe", fns,
                  reads=[f"a_atm{hg % 2}", f"a_vp{sl}", "a_qpT", "a_Sb"],
                  writes=[f"P{bO}"])
                E("dve", [lambda e, hg=hg, sl=sl, bO=bO: e.tensor_tensor(
                            out=a_den[:, hg * 4:hg * 4 + 4].unsqueeze(2),
                            in0=PB[bO][:, 0:260].rearrange(
                                "p (g f) -> p g f", g=4)[:, :, 64:65],
                            in1=a_eps[:, sl, hg * 4:hg * 4 + 4].unsqueeze(2),
                            op=ALU.add)],
                  reads=[f"P{bO}", f"a_eps{sl}"], writes=["a_den"])
                E("dve", [lambda e, hg=hg: e.reciprocal(
                            out=a_den[:, hg * 4:hg * 4 + 4],
                            in_=a_den[:, hg * 4:hg * 4 + 4])],
                  reads=["a_den"], writes=["a_den"])
                E("dve", [lambda e, hg=hg, bO=bO: e.tensor_tensor(
                            out=a_y[:].rearrange("p (h d) -> p h d", h=H)
                                [:, hg * 4:hg * 4 + 4, :],
                            in0=PB[bO][:, 0:260].rearrange(
                                "p (g f) -> p g f", g=4)[:, :, 0:64],
                            in1=a_den[:, hg * 4:hg * 4 + 4].unsqueeze(2)
                                .broadcast_to([128, 4, 64]),
                            op=ALU.mult)],
                  reads=[f"P{bO}", "a_den"], writes=["a_y"])

        for hg in range(4):
            bD = nb()
            fns = []
            for hh in range(4):
                h = hg * 4 + hh
                fns.append(lambda e, h=h: e.ldweights(a_kp[:, h, :]))
                fns.append(lambda e, h=h, hh=hh, sl=sl, bD=bD: e.matmul(
                    PB[bD][:, hh * 65:hh * 65 + 65],
                    a_kp[:, h, :], a_vp[:, sl, h, :],
                    start=(hh == 0), stop=(hh == 3)))
            E("pe", fns, reads=["a_kp", f"a_vp{sl}"], writes=[f"P{bD}"])
            s_lo, s_hi = hg * 4 * 65, (hg * 4 + 4) * 65
            if is_prefix:
                E("dve", [lambda e, s_lo=s_lo, s_hi=s_hi, bD=bD:
                          e.scalar_tensor_tensor(
                            out=a_S[:, s_lo:s_hi], in0=PB[bD][:, 0:260],
                            scalar=c_ps[:, 0:1], in1=a_S[:, s_lo:s_hi],
                            op0=ALU.mult, op1=ALU.add)],
                  reads=[f"P{bD}", "c_ps", "a_S"], writes=["a_S"])
            else:
                E("dve", [lambda e, s_lo=s_lo, s_hi=s_hi, bD=bD: e.tensor_tensor(
                            out=a_S[:, s_lo:s_hi], in0=PB[bD][:, 0:260],
                            in1=a_S[:, s_lo:s_hi], op=ALU.add)],
                  reads=[f"P{bD}", "a_S"], writes=["a_S"])
        E("dve", [lambda e: e.tensor_copy(a_Sb[:], a_S[:])],
          reads=["a_S"], writes=["a_Sb"])

        if is_prefix:
            return

        transpose8(a_y[:], "a_y", a_yT[:], "a_yT", nb())
        wob = [nb(), nb()]
        for half in range(2):
            bank = wob[half]
            fns = []
            for k in range(8):
                fns.append(lambda e, k=k: e.ldweights(a_yT[:, k, :]))
                fns.append(lambda e, k=k, half=half, bank=bank: e.matmul(
                    PB[bank][:], a_yT[:, k, :],
                    a_wo[:, k, half * 512:half * 512 + 512],
                    start=(k == 0), stop=(k == 7)))
            E("pe", fns, reads=["a_yT"] + [f"wo{k}" for k in range(8)],
              writes=[f"P{bank}"])
        for half in range(2):
            bank = wob[half]
            E("act", [lambda e, half=half, bank=bank: e.copy(
                        a_ao[:, half * 512:half * 512 + 512], PB[bank][:])],
              reads=[f"P{bank}"], writes=["a_ao"])
        ln_stats(a_ao[:, 0:D], ["a_ao"], 24)
        E("dve", [lambda e, sl=sl: e.tensor_scalar(
                    out=a_x1[:, sl, :], in0=a_ao[:, 0:D],
                    scalar1=a_st[:, 26:27], scalar2=a_st[:, 28:29],
                    op0=ALU.subtract, op1=ALU.mult)],
          reads=["a_ao", "st26", "st28"], writes=[f"x1.{sl}"])
        E("dve", [lambda e, sl=sl: e.tensor_tensor(
                    out=a_x1[:, sl, :], in0=a_x1[:, sl, :],
                    in1=a_x0[:, sl, :], op=ALU.add)],
          reads=[f"x0.{sl}", f"x1.{sl}"], writes=[f"x1.{sl}"])
        E("sp", [lambda e, cidx=cidx, sl=sl: e.dma_start(
                    out=x1s_t[:, cidx, :], in_=a_x1[:, sl, :])],
          reads=[f"x1.{sl}"], writes=["x1s_hbm"], dma=True)

    seq = [(c, True, xp_t) for c in range(NP)] + \
          [(c, False, xm_t) for c in range(NM)]
    if _os.environ.get("SKIPA") == "1":
        seq = seq[0:1]
    front(seq[0][0], seq[0][1], seq[0][2], 0)
    for i in range(len(seq)):
        for _ in range(2):
            if qsteps:
                qsteps.pop()()
        if i + 1 < len(seq):
            c2, p2, x2 = seq[i + 1]
            front(c2, p2, x2, (i + 1) % 2)
        c1, p1, _ = seq[i]
        back(c1, p1, i % 2)
    while qsteps:
        qsteps.pop()()


    tk.barrier()
    stA.close()

    f_w1 = SB("f_w1", [128, 8, DH], BF16)
    f_w2 = SB("f_w2", [128, 32, D], BF16)
    f_x1 = SB("f_x1", [128, 4, D], F32)
    f_xn = SB("f_xn", [128, D], F32)
    f_xq1 = SB("f_xq1", [128, D], BF16)
    f_xq1T = SB("f_xq1T", [128, 8, 4, 128], BF16)
    f_xq1Tf = f_xq1T[:].rearrange("p a s t -> p a (s t)")
    f_h4 = SB("f_h4", [128, 32, 256], F32)
    f_tail = SB("f_tail", [128, 32], F32)
    f_cv = SB("f_cv", [128, 2, 257], F32)
    f_M = SB("f_M", [128, 256], F32)
    f_s2 = SB("f_s2", [128, 256], F32)
    f_row = SB("f_row", [128, 256], F32)
    f_tok = SB("f_tok", [128, 8], F32)
    f_xq2 = SB("f_xq2", [128, 2, 256], BF16)
    f_qs = SB("f_qs", [128, 2, 256], F32)
    f_qu = SB("f_qu", [128, 2, 256], U8)
    f_qd = SB("f_qd", [128, 2, 256], U8)
    f_y2 = SB("f_y2", [128, D], F32)
    f_st = SB("f_st", [128, 24], F32)
    f_u8 = SB("f_u8", [128, D], U8)
    f_u8b = SB("f_u8b", [128, D], U8)

    E("act", [lambda e: e.preload_act_table(AF.Silu)], writes=["acttab"])
    for k in range(8):
        E("sp", [lambda e, k=k: e.dma_start(
                    out=f_w1[:, k, :], in_=w1qs[k * 128:(k + 1) * 128, :])],
          reads=["wq_hbm"], writes=[f"fw1{k}"], dma=True)
    for m in range(32):
        E("sp", [lambda e, m=m: e.dma_start(
                    out=f_w2[:, m, :], in_=w2qs[m * 128:(m + 1) * 128, :])],
          reads=["wq_hbm"], writes=[f"fw2{m}"], dma=True)
    E("dve", [lambda e: e.memset(f_tail[:], 0.0)], writes=["f_tail"])

    def fln_stats(xin_ap, rbufs, dst_mean, dst_rstd):
        E("dve", [lambda e: e.bn_stats(f_st[:, 8:14], xin_ap[:, 0:512])],
          reads=rbufs, writes=["fst8"])
        E("dve", [lambda e: e.bn_stats(f_st[:, 14:20], xin_ap[:, 512:1024])],
          reads=rbufs, writes=["fst14"])
        E("dve", [lambda e: e.bn_aggr(f_st[:, 2:4], f_st[:, 8:20])],
          reads=["fst8", "fst14"], writes=["fstm", "fst3"])
        E("dve", [lambda e: e.tensor_scalar(out=f_st[:, 4:5], in0=f_st[:, 3:4],
                                            scalar1=EPS_LN, scalar2=None,
                                            op0=ALU.add)],
          reads=["fst3"], writes=["fst4"])
        E("dve", [lambda e: e.tensor_scalar(
                    out=f_st[:, 5:6].bitcast(mybir.dt.int32)[:, 0:1],
                    in0=f_st[:, 4:5].bitcast(mybir.dt.int32)[:, 0:1],
                    scalar1=1, scalar2=-1, op0=ALU.logical_shift_right,
                    op1=ALU.bitwise_xor)],
          reads=["fst4"], writes=["fst5"])
        E("dve", [lambda e: e.tensor_scalar(
                    out=f_st[:, 5:6].bitcast(mybir.dt.int32)[:, 0:1],
                    in0=f_st[:, 5:6].bitcast(mybir.dt.int32)[:, 0:1],
                    scalar1=0x5f3759e0, scalar2=None, op0=ALU.add)],
          reads=["fst5"], writes=["fst5"])
        for _ in range(3):
            E("dve", [lambda e: e.tensor_tensor(out=f_st[:, 6:7],
                                                in0=f_st[:, 5:6],
                                                in1=f_st[:, 5:6],
                                                op=ALU.mult)],
              reads=["fst5"], writes=["fst6"])
            E("dve", [lambda e: e.tensor_tensor(out=f_st[:, 6:7],
                                                in0=f_st[:, 6:7],
                                                in1=f_st[:, 4:5],
                                                op=ALU.mult)],
              reads=["fst6", "fst4"], writes=["fst6"])
            E("dve", [lambda e: e.tensor_scalar(out=f_st[:, 6:7],
                                                in0=f_st[:, 6:7],
                                                scalar1=-0.5, scalar2=1.5,
                                                op0=ALU.mult, op1=ALU.add)],
              reads=["fst6"], writes=["fst6"])
            E("dve", [lambda e: e.tensor_tensor(out=f_st[:, 5:6],
                                                in0=f_st[:, 5:6],
                                                in1=f_st[:, 6:7],
                                                op=ALU.mult)],
              reads=["fst5", "fst6"], writes=["fst5"])
        E("dve", [lambda e: e.tensor_copy(f_st[:, dst_rstd:dst_rstd + 1],
                                          f_st[:, 5:6])],
          reads=["fst5"], writes=["fstr"])

    n_tiles = (NM + 1) // 2

    def prep_chunk(cidx):
        slot = cidx % 4
        xsl = cidx % 4
        E("sp", [lambda e, cidx=cidx, xsl=xsl: e.dma_start(
                    out=f_x1[:, xsl, :], in_=x1s_t[:, cidx, :])],
          reads=["x1s_hbm"], writes=[f"fx1.{xsl}"], dma=True)
        fln_stats(f_x1[:, xsl, :], [f"fx1.{xsl}"], 2, 7)
        E("dve", [lambda e, xsl=xsl: e.tensor_scalar(
                    out=f_xn[:], in0=f_x1[:, xsl, :],
                    scalar1=f_st[:, 2:3], scalar2=f_st[:, 7:8],
                    op0=ALU.subtract, op1=ALU.mult)],
          reads=[f"fx1.{xsl}", "fstm", "fstr"], writes=["f_xn"])
        E("dve", [lambda e: e.tensor_reduce(out=f_st[:, 8:9], in_=f_xn[:, 0:D],
                                            axis=AX.X, op=ALU.max,
                                            apply_absolute_value=True)],
          reads=["f_xn"], writes=["fst8"])
        E("dve", [lambda e: e.tensor_scalar(out=f_st[:, 8:9], in0=f_st[:, 8:9],
                                            scalar1=1e-5, scalar2=None,
                                            op0=ALU.max)],
          reads=["fst8"], writes=["fst8"])
        E("dve", [lambda e: e.reciprocal(out=f_st[:, 9:10], in_=f_st[:, 8:9])],
          reads=["fst8"], writes=["fst9"])
        E("dve", [lambda e: e.tensor_scalar(out=f_st[:, 9:10],
                                            in0=f_st[:, 9:10],
                                            scalar1=127.0, scalar2=None,
                                            op0=ALU.mult)],
          reads=["fst9"], writes=["fst9"])
        E("dve", [lambda e: e.tensor_scalar(out=f_st[:, 10:11],
                                            in0=f_st[:, 8:9],
                                            scalar1=c_sw[:, 0:1],
                                            scalar2=1.0 / 127.0,
                                            op0=ALU.mult, op1=ALU.mult)],
          reads=["fst8", "c_sw0"], writes=["fst10"])
        E("dve", [lambda e: e.tensor_scalar(out=f_xn[:], in0=f_xn[:],
                                            scalar1=f_st[:, 9:10],
                                            scalar2=MAGIC,
                                            op0=ALU.mult, op1=ALU.add)],
          reads=["f_xn", "fst9"], writes=["f_xn"])
        E("dve", [lambda e: e.tensor_copy(f_u8[:], f_xn[:])],
          reads=["f_xn"], writes=["f_u8"])
        E("dve", [lambda e: e.tensor_tensor(out=f_u8b[:], in0=f_u8[:],
                                            in1=f_xn[:], op=ALU.is_gt)],
          reads=["f_u8", "f_xn"], writes=["f_u8b"])
        E("dve", [lambda e: e.tensor_tensor(out=f_xn[:], in0=f_u8[:],
                                            in1=f_u8b[:], op=ALU.subtract)],
          reads=["f_u8", "f_u8b"], writes=["f_xn"])
        E("dve", [lambda e: e.tensor_scalar(out=f_xq1[:], in0=f_xn[:],
                                            scalar1=-128.0,
                                            scalar2=f_st[:, 10:11],
                                            op0=ALU.add, op1=ALU.mult)],
          reads=["f_xn", "fst10"], writes=["f_xq1"])
        fns = []
        for k in range(8):
            fns.append(lambda e, k=k: e.transpose(
                PBb[2][:, k * 128:(k + 1) * 128],
                f_xq1[:, k * 128:(k + 1) * 128], c_ident[:]))
        E("pe", fns, reads=["f_xq1", "c_ident"], writes=["P2"])
        E("dve", [lambda e, slot=slot: e.tensor_copy(
                    f_xq1T[:, :, slot, :],
                    PBb[2][:].rearrange("p (a b) -> p a b", a=8))],
          reads=["P2"], writes=[f"xq1T.{slot}"])

    def ffn_tile(ft):
        c0 = ft * 2
        ntok = 256 if c0 + 1 < NM else 128
        nch = 2 if ntok == 256 else 1
        slot0 = c0 % 4
        halo_slot = (c0 + 2) % 4 if c0 + 2 < NM else None
        rbufs = [f"xq1T.{slot0}"] + ([f"xq1T.{slot0 + 1}"] if nch == 2 else [])
        if halo_slot is not None:
            rbufs.append(f"xq1T.{halo_slot}")
        for m in range(32):
            hb = m % 3
            fns = []
            for k in range(8):
                fns.append(lambda e, k=k, m=m: e.ldweights(
                    f_w1[:, k, m * 128:(m + 1) * 128]))
                fns.append(lambda e, k=k, m=m, hb=hb, slot0=slot0, ntok=ntok:
                           e.matmul(PB[hb][:, 0:ntok],
                                    f_w1[:, k, m * 128:(m + 1) * 128],
                                    f_xq1Tf[:, k, slot0 * 128:
                                            slot0 * 128 + ntok],
                                    start=(k == 0), stop=(k == 7 and
                                                          halo_slot is None)))
            if halo_slot is not None:
                for k in range(8):
                    fns.append(lambda e, k=k, m=m, hb=hb, halo_slot=halo_slot,
                               ntok=ntok: e.matmul(
                                   PB[hb][:, ntok:ntok + 1],
                                   f_w1[:, k, m * 128:(m + 1) * 128],
                                   f_xq1Tf[:, k, halo_slot * 128:
                                           halo_slot * 128 + 1],
                                   start=False, stop=(k == 7)))
            E("pe", fns, reads=rbufs + [f"fw1{k}" for k in range(8)],
              writes=[f"P{hb}"])
            nconv = ntok + (1 if halo_slot is not None else 0)
            E("act", [lambda e, m=m, hb=hb, nconv=nconv: e.activation(
                        out=f_cv[:, 0, 0:nconv], in_=PB[hb][:, 0:nconv],
                        func=AF.Identity, bias=c_conv[:, m * 4 + 3:m * 4 + 4],
                        scale=c_conv[:, m * 4 + 1:m * 4 + 2])],
              reads=[f"P{hb}", "c_conv"], writes=["f_cv0"])
            E("dve", [lambda e, m=m, hb=hb, ntok=ntok: e.scalar_tensor_tensor(
                        out=f_cv[:, 1, 1:ntok], in0=PB[hb][:, 0:ntok - 1],
                        scalar=c_conv[:, m * 4:m * 4 + 1],
                        in1=f_cv[:, 0, 1:ntok], op0=ALU.mult, op1=ALU.add)],
              reads=[f"P{hb}", "c_conv", "f_cv0"], writes=["f_cv1"])
            E("dve", [lambda e, m=m: e.scalar_tensor_tensor(
                        out=f_cv[:, 1, 0:1], in0=f_tail[:, m:m + 1],
                        scalar=c_conv[:, m * 4:m * 4 + 1],
                        in1=f_cv[:, 0, 0:1], op0=ALU.mult, op1=ALU.add)],
              reads=["f_tail", "c_conv", "f_cv0"], writes=["f_cv1"])
            E("dve", [lambda e, m=m, hb=hb, ntok=ntok, nconv=nconv:
                      e.scalar_tensor_tensor(
                        out=f_cv[:, 0, 0:ntok - 1], in0=PB[hb][:, 1:ntok],
                        scalar=c_conv[:, m * 4 + 2:m * 4 + 3],
                        in1=f_cv[:, 1, 0:ntok - 1], op0=ALU.mult, op1=ALU.add)],
              reads=[f"P{hb}", "c_conv", "f_cv1"], writes=["f_cv0"])
            if halo_slot is not None:
                E("dve", [lambda e, m=m, hb=hb, ntok=ntok:
                          e.scalar_tensor_tensor(
                            out=f_cv[:, 0, ntok - 1:ntok],
                            in0=PB[hb][:, ntok:ntok + 1],
                            scalar=c_conv[:, m * 4 + 2:m * 4 + 3],
                            in1=f_cv[:, 1, ntok - 1:ntok],
                            op0=ALU.mult, op1=ALU.add)],
                  reads=[f"P{hb}", "c_conv", "f_cv1"], writes=["f_cv0"])
            else:
                E("dve", [lambda e, ntok=ntok: e.tensor_copy(
                            f_cv[:, 0, ntok - 1:ntok],
                            f_cv[:, 1, ntok - 1:ntok])],
                  reads=["f_cv1"], writes=["f_cv0"])
            E("dve", [lambda e, m=m, hb=hb, ntok=ntok: e.tensor_copy(
                        f_tail[:, m:m + 1], PB[hb][:, ntok - 1:ntok])],
              reads=[f"P{hb}"], writes=["f_tail"])
            E("act", [lambda e, m=m, ntok=ntok: e.activation(
                        out=f_h4[:, m, 0:ntok], in_=f_cv[:, 0, 0:ntok],
                        func=AF.Silu)],
              reads=["f_cv0"], writes=[f"h4.{m}", "h4x"])

        E("dve", [lambda e, ntok=ntok: e.tensor_reduce(
                    out=f_M[:, 0:ntok],
                    in_=f_h4[:][:, :, 0:ntok].transpose([0, 2, 1]),
                    axis=AX.X, op=ALU.max, apply_absolute_value=True)],
          reads=[f"h4x"], writes=["f_M"])
        for b in range(nch):
            E("pe", [lambda e, b=b: e.transpose(
                        PB[7][:, 0:128],
                        f_M[:, b * 128:(b + 1) * 128], c_identf[:])],
              reads=["f_M", "c_identf"], writes=["P7"])
            E("dve", [lambda e, b=b: e.tensor_reduce(
                        out=f_tok[:, 4 + b:5 + b], in_=PB[7][:, 0:128],
                        axis=AX.X, op=ALU.max)],
              reads=["P7"], writes=["f_tokM"])
            E("dve", [lambda e, b=b: e.tensor_scalar(
                        out=f_tok[:, 4 + b:5 + b], in0=f_tok[:, 4 + b:5 + b],
                        scalar1=1e-5, scalar2=None, op0=ALU.max)],
              reads=["f_tokM"], writes=["f_tokM"])
            E("dve", [lambda e, b=b: e.reciprocal(
                        out=f_tok[:, 2 + b:3 + b], in_=f_tok[:, 4 + b:5 + b])],
              reads=["f_tokM"], writes=["f_tokS"])
            E("dve", [lambda e, b=b: e.tensor_scalar(
                        out=f_tok[:, 2 + b:3 + b], in0=f_tok[:, 2 + b:3 + b],
                        scalar1=127.0, scalar2=None, op0=ALU.mult)],
              reads=["f_tokS"], writes=["f_tokS"])
            E("dve", [lambda e, b=b: e.tensor_scalar(
                        out=f_tok[:, b:b + 1], in0=f_tok[:, 4 + b:5 + b],
                        scalar1=c_sw[:, 1:2], scalar2=1.0 / 127.0,
                        op0=ALU.mult, op1=ALU.mult)],
              reads=["f_tokM", "c_sw1"], writes=["f_tok"])
            E("pe", [lambda e, b=b: e.matmul(
                        PB[7][0:1, 0:128], f_tok[:, 2 + b:3 + b], c_identf[:],
                        start=True, stop=True)],
              reads=["f_tokS", "c_identf"], writes=["P7"])
            E("dve", [lambda e, b=b: e.tensor_copy(
                        f_row[0:1, b * 128:(b + 1) * 128], PB[7][0:1, 0:128])],
              reads=["P7"], writes=["f_rowS"])
        E("pe", [lambda e, ntok=ntok: e.matmul(
                    PB[7][:, 0:ntok], c_onesf[0:1, :], f_row[0:1, 0:ntok],
                    start=True, stop=True)],
          reads=["f_rowS", "c_onesf"], writes=["P7"])
        E("dve", [lambda e, ntok=ntok: e.tensor_copy(
                    f_s2[:, 0:ntok], PB[7][:, 0:ntok])],
          reads=["P7"], writes=["f_s2"])

        for m in range(32):
            xsl = m % 2
            E("dve", [lambda e, m=m, ntok=ntok, xsl=xsl: e.tensor_tensor(
                        out=f_qs[:, xsl, 0:ntok], in0=f_h4[:, m, 0:ntok],
                        in1=f_s2[:, 0:ntok], op=ALU.mult)],
              reads=[f"h4.{m}", "f_s2"], writes=[f"f_qs{xsl}"])
            E("dve", [lambda e, ntok=ntok, xsl=xsl: e.tensor_scalar(
                        out=f_qs[:, xsl, 0:ntok], in0=f_qs[:, xsl, 0:ntok],
                        scalar1=MAGIC, scalar2=None, op0=ALU.add)],
              reads=[f"f_qs{xsl}"], writes=[f"f_qs{xsl}"])
            E("dve", [lambda e, ntok=ntok, xsl=xsl: e.tensor_copy(
                        f_qu[:, xsl, 0:ntok], f_qs[:, xsl, 0:ntok])],
              reads=[f"f_qs{xsl}"], writes=[f"f_qu{xsl}"])
            E("dve", [lambda e, ntok=ntok, xsl=xsl: e.tensor_tensor(
                        out=f_qd[:, xsl, 0:ntok], in0=f_qu[:, xsl, 0:ntok],
                        in1=f_qs[:, xsl, 0:ntok], op=ALU.is_gt)],
              reads=[f"f_qu{xsl}", f"f_qs{xsl}"], writes=[f"f_qd{xsl}"])
            E("dve", [lambda e, xsl=xsl, ntok=ntok: e.scalar_tensor_tensor(
                        out=f_xq2[:, xsl, 0:ntok], in0=f_qu[:, xsl, 0:ntok],
                        scalar=-128.0, in1=f_qd[:, xsl, 0:ntok],
                        op0=ALU.add, op1=ALU.subtract)],
              reads=[f"f_qu{xsl}", f"f_qd{xsl}"], writes=[f"xq2.{xsl}"])
            for s in range(nch):
                bank = 3 + 2 * s
                fns = []
                fns.append(lambda e, m=m, xsl=xsl, s=s: e.ldweights(
                    f_xq2[:, xsl, s * 128:(s + 1) * 128]))
                for n in range(2):
                    fns.append(lambda e, m=m, xsl=xsl, s=s, n=n, bank=bank:
                               e.matmul(
                                   PB[bank + n][:],
                                   f_xq2[:, xsl, s * 128:(s + 1) * 128],
                                   f_w2[:, m, n * 512:(n + 1) * 512],
                                   start=(m == 0), stop=(m == 31)))
                E("pe", fns,
                  reads=[f"xq2.{xsl}", f"fw2{m}"],
                  writes=[f"P{bank}", f"P{bank + 1}"])
        if DBG and ft == 0:
            dump(8, ap_f32=f_x1[:, 0, :], rtag="fx1.0")
            dump(9, ap_bf16=f_xq1T[:, 0:4, 0:2, :], rtag=["xq1T.0", "xq1T.1"])
            dump(10, ap_f32=f_h4[:, 0, 0:256], rtag="h4.0")
            dump(11, ap_f32=f_h4[:, 17, 0:256], rtag="h4.17")
            dump(12, ap_f32=f_M[:, 0:256], rtag="f_M")
            dump(13, ap_f32=f_s2[:, 0:256], rtag="f_s2")
            dump(14, ap_f32=f_tok[:, 0:6], rtag=["f_tok", "f_tokS", "f_tokM"])
        for s in range(nch):
            bank = 3 + 2 * s
            xsl = (c0 + s) % 4
            for n in range(2):
                E("dve", [lambda e, s=s, n=n, bank=bank: e.tensor_scalar(
                            out=f_y2[:, n * 512:(n + 1) * 512],
                            in0=PB[bank + n][:],
                            scalar1=f_tok[:, s:s + 1], scalar2=None,
                            op0=ALU.mult)],
                  reads=[f"P{bank + n}", "f_tok"], writes=["y2"])
            if DBG and ft == 0 and s == 0:
                dump(15, ap_f32=f_y2[:], rtag="y2")
            fln_stats(f_y2[:, 0:D], ["y2"], 2, 7)
            E("dve", [lambda e: e.tensor_scalar(
                        out=f_y2[:], in0=f_y2[:],
                        scalar1=f_st[:, 2:3], scalar2=f_st[:, 7:8],
                        op0=ALU.subtract, op1=ALU.mult)],
              reads=["y2", "fstm", "fstr"], writes=["y2"])
            E("dve", [lambda e, xsl=xsl: e.tensor_tensor(
                        out=f_x1[:, xsl, :], in0=f_x1[:, xsl, :],
                        in1=f_y2[:], op=ALU.add)],
              reads=["y2", f"fx1.{xsl}"], writes=[f"fx1.{xsl}"])
            E("sp", [lambda e, s=s, c0=c0, xsl=xsl: e.dma_start(
                        out=y_t[:, c0 + s, :], in_=f_x1[:, xsl, :])],
              reads=[f"fx1.{xsl}"], writes=["y_hbm"], dma=True)

    prep_chunk(0)
    if NM > 1:
        prep_chunk(1)
    if _os.environ.get("SKIPF") == "1":
        n_tiles = 1
    for ft in range(n_tiles):
        nxt = ft * 2 + 2
        if nxt < NM:
            prep_chunk(nxt)
        if nxt + 1 < NM:
            prep_chunk(nxt + 1)
        ffn_tile(ft)

    tk.barrier()
    tk.close()
    st.close()
    return nc


def _host_inputs(inputs, NM, NP, core):
    b, j = core // 2, core % 2
    x = inputs["L0_attn_x"]
    TM, TP = NM * CH, NP * CH
    if j == 0:
        xm = x[b, 0:TM, :]
    else:
        xm = x[b, x.shape[1] - TM:, :]
    xp = x[b, 0:max(TP, 1), :]
    wqkv = np.concatenate([np.asarray(inputs["L0_attn_Wq"]),
                           np.asarray(inputs["L0_attn_Wk"]),
                           np.asarray(inputs["L0_attn_Wv"])], axis=1)
    wf = np.asarray(inputs["L0_attn_W_feat"])
    wfeat2 = np.concatenate([wf, wf], axis=0)
    cw = np.asarray(inputs["L0_ffn_conv_w"]).reshape(DH, 3)
    cb = np.asarray(inputs["L0_ffn_conv_b"]).reshape(DH)
    convpk = np.zeros((128, 128), dtype=np.float32)
    for m in range(32):
        convpk[:, m * 4 + 0] = cw[m * 128:(m + 1) * 128, 0]
        convpk[:, m * 4 + 1] = cw[m * 128:(m + 1) * 128, 1]
        convpk[:, m * 4 + 2] = cw[m * 128:(m + 1) * 128, 2]
        convpk[:, m * 4 + 3] = cb[m * 128:(m + 1) * 128]
    import ml_dtypes
    ident = np.eye(128, dtype=ml_dtypes.bfloat16)
    maskT = np.triu(np.ones((128, 128), dtype=ml_dtypes.bfloat16))
    pscale = np.full((128, 1), 1.0 if j == 1 else 0.0, dtype=np.float32)
    biasc = np.zeros((128, 8), dtype=np.float32)
    biasc[:, 0] = EPS_LN
    biasc[:, 1] = K0 - LSQ
    biasc[:, 2] = -K0 + math.log(EPS_ATT) + LSQ
    biasc[:, 3] = -K0
    biasc[:, 4] = MAGIC
    return {
        "x_main": np.ascontiguousarray(xm, dtype=np.float32),
        "x_pref": np.ascontiguousarray(xp, dtype=np.float32),
        "wqkv": np.ascontiguousarray(wqkv, dtype=np.float32),
        "wo": np.ascontiguousarray(inputs["L0_attn_Wo"], dtype=np.float32),
        "wfeat2": np.ascontiguousarray(wfeat2, dtype=np.float32),
        "w1T": np.ascontiguousarray(np.asarray(inputs["L0_ffn_w1"]).T,
                                    dtype=np.float32),
        "w2T": np.ascontiguousarray(np.asarray(inputs["L0_ffn_w2"]).T,
                                    dtype=np.float32),
        "convpk": convpk,
        "ident": ident,
        "identf": np.eye(128, dtype=np.float32),
        "onesf": np.ones((128, 128), dtype=np.float32),
        "maskT": maskT,
        "pscale": pscale,
        "biasc": biasc,
    }


def kernel(**inputs):
    NM, NP = FULL["NM"], FULL["NP"]
    nc = bass.Bass("TRN2", target_bir_lowering=False)
    build_program(nc, NM, NP)
    in_maps = [_host_inputs(inputs, NM, NP, c) for c in range(8)]
    res = run_bass_kernel_spmd(nc, in_maps, core_ids=list(range(8)))
    T = 8192
    out = np.zeros((4, T, D), dtype=np.float32)
    for c in range(8):
        b, j = c // 2, c % 2
        yc = res.results[c]["y"]
        yc = yc.reshape(NM * CH, D)
        if j == 0:
            out[b, 0:T0, :] = yc[0:T0, :]
        else:
            out[b, T0:, :] = yc[NM * CH - (T - T0):, :]
    return out
